# version 25
# baseline (speedup 1.0000x reference)
import numpy as np
import concourse.bass as bass
import concourse.mybir as mybir
from concourse.bass_utils import run_bass_kernel_spmd

F32 = mybir.dt.float32
BF16 = mybir.dt.bfloat16
I32 = mybir.dt.int32
AF = mybir.ActivationFunctionType
ALU = mybir.AluOpType
AX = mybir.AxisListType

MAGIC = 12582912.0
T = 2048
D = 1024
E = 3072
I_ = 2816
KD = D // 128
KE = E // 128
KI = I_ // 128
NT = T // 128
T4 = T // 512
NCORES = 8
QMAX = 127.0


class Sched:

    ENGS = ("sync", "vector", "scalar", "tensor", "gpsimd")

    def __init__(self):
        self.ops = {e: [] for e in self.ENGS}
        self.cnt = {}
        self.semh = {}

    def inc(self, sem, by):
        self.cnt[sem] = self.cnt.get(sem, 0) + by
        return self.cnt[sem]

    def now(self, sem):
        return self.cnt.get(sem, 0)

    def emit(self, eng, mk, sem=None, by=1):
        if sem is None:
            self.ops[eng].append(lambda e, mk=mk: mk(e))
        else:
            self.inc(sem, by)
            self.ops[eng].append(
                lambda e, mk=mk, s=sem, b=by: mk(e).then_inc(self.semh[s], b))

    def dma(self, eng, mk):
        sem = {"sync": "si", "scalar": "ss"}.get(eng, "sg")
        self.emit(eng, mk, sem=sem, by=16)
        return self.now(sem)

    def wait(self, eng, sem, val):
        if val <= 0:
            return
        self.ops[eng].append(
            lambda e, s=sem, v=val: e.wait_ge(self.semh[s], v))

    def wait_now(self, eng, sem):
        self.wait(eng, sem, self.now(sem))

    def slow(self, eng, mk, sem="v"):
        self.emit(eng, lambda e: e.engine_nop(), sem=sem)
        self.wait_now(eng, sem)
        self.emit(eng, mk, sem=sem)
        self.wait_now(eng, sem)

    def barrier(self, engines=None, sems=("si", "sg", "ss", "v", "s", "t")):
        engines = engines or self.ENGS
        vals = {s: self.now(s) for s in sems}
        for e in engines:
            for s, v in vals.items():
                self.wait(e, s, v)


class Pool:
    def __init__(self, nc, side="left"):
        self.nc = nc
        self.side = side
        self.guards = []

    def sb(self, name, shape, dt=F32):
        g = self.nc.sbuf_tensor(name, shape, dt, side=self.side)
        h = g.__enter__()
        self.guards.append(g)
        return h

    def ps(self, name, shape):
        g = self.nc.psum_tensor(name, shape, F32)
        h = g.__enter__()
        self.guards.append(g)
        return h

    def free(self):
        for g in reversed(self.guards):
            g.__exit__(None, None, None)
        self.guards = []


def build():
    nc = bass.Bass("TRN2", target_bir_lowering=False, detect_race_conditions=False)

    x_d = nc.dram_tensor("L0_attn_x", [T, D], F32, kind="ExternalInput").ap()
    wnorm_d = nc.dram_tensor("L0_attn_w_norm", [D], F32, kind="ExternalInput").ap()
    win_d = nc.dram_tensor("L0_attn_W_in", [KE, 128, KD, 128], F32, kind="ExternalInput").ap()
    lam_d = nc.dram_tensor("L0_attn_lambda_log", [D], F32, kind="ExternalInput").ap()
    wout_d = nc.dram_tensor("L0_attn_W_out", [D, D], F32, kind="ExternalInput").ap()
    clipx_d = nc.dram_tensor("L0_ffn_clip_x", [1], F32, kind="ExternalInput").ap()
    clipd_d = nc.dram_tensor("L0_ffn_clip_down", [1], F32, kind="ExternalInput").ap()
    wg_d = nc.dram_tensor("L0_ffn_w_gate", [I_, D], F32, kind="ExternalInput").ap()
    wu_d = nc.dram_tensor("L0_ffn_w_up", [I_, D], F32, kind="ExternalInput").ap()
    wd_d = nc.dram_tensor("L0_ffn_w_down", [D, I_], F32, kind="ExternalInput").ap()
    out_d = nc.dram_tensor("out", [T, D], F32, kind="ExternalOutput").ap()

    hsp_d = nc.dram_tensor("h_spill", [T // 256, 128, KI * 256], F32).ap()
    asp_d = nc.dram_tensor("attn_spill", [NT, 128, D], F32).ap()
    red_d = nc.dram_tensor("red_bounce", [128, 1], F32).ap()
    os_d = nc.dram_tensor("oscale_bounce", [KD, 128], F32).ap()
    cc1i = nc.dram_tensor("cc1_in", [1, 1], F32)
    cc1o = nc.dram_tensor("cc1_out", [1, 1], F32)
    cc2i = nc.dram_tensor("cc2_in", [1, 1], F32)
    cc2o = nc.dram_tensor("cc2_out", [1, 1], F32)

    sc = Sched()

    perm = Pool(nc)
    stats_negm = perm.sb("stats_negm", [128, NT])
    stats_rinv = perm.sb("stats_rinv", [128, NT])
    stats_r3 = perm.sb("stats_r3", [128, NT])
    cmax16 = perm.sb("cmax16", [128, NT])
    wn_bc = perm.sb("wn_bc", [128, D])
    oscale_bc = perm.sb("oscale_bc", [128, D])
    lam_col = perm.sb("lam_col", [128, KD])
    comb_g = perm.sb("comb_g", [128, KI])
    comb_u = perm.sb("comb_u", [128, KI])
    comb_d = perm.sb("comb_d", [128, KD])
    drmax8_p = perm.sb("drmax8_p", [128, KD])
    inv_sx_col = perm.sb("inv_sx_col", [128, 1])
    sx_col = perm.sb("sx_col", [128, 1])
    inv_sh_col = perm.sb("inv_sh_col", [128, 1])
    sh_col = perm.sb("sh_col", [128, 1])
    gm1 = perm.sb("gm1", [1, 1])
    gmx_col = perm.sb("gmx_col", [128, 1])
    sg_col = perm.sb("sg_col", [128, 1])
    clipx_col = perm.sb("clipx_col", [128, 1])
    clipd_col = perm.sb("clipd_col", [128, 1])
    pm_row = perm.sb("pm_row", [1, 128])

    xq_dram = nc.dram_tensor("xq_spill", [128, KD * T], BF16).ap()

    poolA = Pool(nc)
    xrh_fm = poolA.sb("xrh_fm", [128, KD * T], BF16)
    xrl_fm = poolA.sb("xrl_fm", [128, KD * T], BF16)

    p0 = Pool(nc, side="right")
    xq_fm = p0.sb("xq_fm", [128, KD * T], BF16)
    xc_full = p0.sb("xc_full", [128, NT * D])
    xst = [p0.sb(f"xst{i}", [128, D]) for i in range(2)]
    ssq = p0.sb("ssq", [128, NT])
    sumc = p0.sb("sumc", [128, 2])
    st16a = p0.sb("st16a", [128, NT])
    st16b = p0.sb("st16b", [128, NT])
    st16c = p0.sb("st16c", [128, NT])
    st16d = p0.sb("st16d", [128, NT])
    xrw = [p0.sb(f"xrw{i}", [128, D]) for i in range(2)]
    xrh_t = [p0.sb(f"xrh_t{i}", [128, D], BF16) for i in range(2)]
    xrl_t = [p0.sb(f"xrl_t{i}", [128, D], BF16) for i in range(2)]
    sqscr = xrw[0]
    q0t = xst[0]
    qbt = [p0.sb(f"qbt{i}", [128, D], BF16) for i in range(2)]
    qcol = p0.sb("qcol", [128, NT])
    pmax1 = p0.sb("pmax1", [128, 1])

    sc.dma("sync", lambda e: e.dma_start(
        out=wn_bc[:, :], in_=wnorm_d.rearrange("(a d) -> a d", a=1).broadcast_to([128, D])))
    sc.dma("sync", lambda e: e.dma_start(
        out=clipx_col[:, :],
        in_=clipx_d.rearrange("(a b) -> a b", a=1).broadcast_to([128, 1])))
    sc.dma("sync", lambda e: e.dma_start(
        out=clipd_col[:, :],
        in_=clipd_d.rearrange("(a b) -> a b", a=1).broadcast_to([128, 1])))
    sc.dma("sync", lambda e: e.dma_start(
        out=lam_col[:, :], in_=lam_d.rearrange("(c p) -> p c", p=128),
        allow_slow_non_contiguous=True))

    vdone_p0 = [0] * NT
    sdone_p0 = [0] * NT
    for j in range(NT):
        if j >= 2:
            sc.wait("sync", "s", sdone_p0[j - 2])
        ld = sc.dma("sync", lambda e, j=j: e.dma_start(
            out=xst[j % 2][:, :], in_=x_d[j * 128:(j + 1) * 128, :]))
        sc.wait("vector", "si", ld)
        sc.slow("vector", lambda e, j=j: e.reduce_sum(
            sumc[:, j % 2:j % 2 + 1], xst[j % 2][:, :], axis=AX.X))
        sc.emit("vector", lambda e, j=j: e.tensor_single_scalar(
            stats_negm[:, j:j + 1], sumc[:, j % 2:j % 2 + 1], -1.0 / D, ALU.mult),
            sem="v")
        vdone_p0[j] = sc.now("v")
        sc.wait("scalar", "v", vdone_p0[j])
        sc.emit("scalar", lambda e, j=j: e.activation(
            xc_full[:, j * D:(j + 1) * D], xst[j % 2][:, :], AF.Identity,
            bias=stats_negm[:, j:j + 1]))
        sc.emit("scalar", lambda e, j=j: e.activation(
            sqscr[:, :], xc_full[:, j * D:(j + 1) * D], AF.Square,
            accum_out=ssq[:, j:j + 1]), sem="s")
        sdone_p0[j] = sc.now("s")
        sc.wait("vector", "s", sdone_p0[j])
        sc.slow("vector", lambda e, j=j: e.reduce_max(
            cmax16[:, j:j + 1], xc_full[:, j * D:(j + 1) * D], axis=AX.X,
            apply_absolute_value=True))

    sc.wait_now("vector", "s")
    sc.emit("vector", lambda e: e.tensor_scalar(
        st16a[:, :], ssq[:, :], 1.0 / D, 1e-5, ALU.mult, ALU.add))
    sc.slow("vector", lambda e: e.reciprocal(st16b[:, :], st16a[:, :]))
    _v = sc.now("v")
    sc.wait("scalar", "v", _v)
    sc.emit("scalar", lambda e: e.activation(
        stats_rinv[:, :], st16b[:, :], AF.Sqrt), sem="s")
    sc.wait("vector", "v", _v)
    sc.emit("vector", lambda e: e.tensor_scalar(
        st16c[:, :], st16a[:, :], -1e-5, None, ALU.add), sem="v")
    _v = sc.now("v")
    sc.wait("scalar", "v", _v)
    sc.emit("scalar", lambda e: e.activation(
        st16d[:, :], st16c[:, :], AF.Ln))
    sc.emit("scalar", lambda e: e.activation(
        st16c[:, :], st16a[:, :], AF.Ln), sem="s")
    sc.wait_now("vector", "s")
    sc.emit("vector", lambda e: e.scalar_tensor_tensor(
        st16c[:, :], st16d[:, :], 1.0, st16c[:, :],
        ALU.mult, ALU.subtract), sem="v")
    sc.wait_now("scalar", "v")
    sc.emit("scalar", lambda e: e.activation(
        st16c[:, :], st16c[:, :], AF.Exp), sem="s")
    sc.wait_now("vector", "s")
    sc.emit("vector", lambda e: e.tensor_scalar(
        st16c[:, :], st16c[:, :], 1e-6, None, ALU.add), sem="v")
    sc.wait_now("scalar", "v")
    sc.emit("scalar", lambda e: e.activation(
        st16c[:, :], st16c[:, :], AF.Ln))
    sc.emit("scalar", lambda e: e.activation(
        st16c[:, :], st16c[:, :], AF.Exp, scale=-0.5), sem="s")
    sc.wait_now("vector", "s")
    sc.emit("vector", lambda e: e.scalar_tensor_tensor(
        stats_r3[:, :], stats_rinv[:, :], 1.0, st16c[:, :],
        ALU.mult, ALU.mult), sem="v")
    r3_done = sc.now("v")

    xr_tp_arr = [0] * NT
    vdone_xr = [0] * NT
    for j in range(NT):
        sc.wait("vector", "v", r3_done)
        sc.emit("vector", lambda e, j=j: e.tensor_scalar(
            xrw[j % 2][:, :], xc_full[:, j * D:(j + 1) * D],
            stats_r3[:, j:j + 1], None, ALU.mult))
        sc.emit("vector", lambda e, j=j: e.tensor_tensor(
            xrw[j % 2][:, :], xrw[j % 2][:, :], wn_bc[:, :], ALU.mult), sem="v")
        _v = sc.now("v")
        sc.wait("gpsimd", "v", _v)
        if j >= 2:
            sc.wait("gpsimd", "si", xr_tp_arr[j - 2])
        cast = sc.dma("gpsimd", lambda e, j=j: e.dma_start(
            out=xrh_t[j % 2][:, :], in_=xrw[j % 2][:, :]))
        sc.wait("vector", "sg", cast)
        if j >= 2:
            sc.wait("vector", "si", xr_tp_arr[j - 2])
        sc.emit("vector", lambda e, j=j: e.tensor_tensor(
            xrl_t[j % 2][:, :], xrw[j % 2][:, :], xrh_t[j % 2][:, :],
            ALU.subtract), sem="v")
        vdone_xr[j] = sc.now("v")
        sc.wait("sync", "v", vdone_xr[j])
        sc.dma("sync", lambda e, j=j: e.dma_start_transpose(
            xrh_fm[:, :].rearrange("p (c t) -> p c t", c=KD)[:, :, j * 128:(j + 1) * 128],
            xrh_t[j % 2][:, :].rearrange("t (c d) -> t c d", c=KD)))
        xr_tp_arr[j] = sc.dma("sync", lambda e, j=j: e.dma_start_transpose(
            xrl_fm[:, :].rearrange("p (c t) -> p c t", c=KD)[:, :, j * 128:(j + 1) * 128],
            xrl_t[j % 2][:, :].rearrange("t (c d) -> t c d", c=KD)))
    xr_tp_done = xr_tp_arr[NT - 1]

    sc.wait_now("vector", "v")
    sc.emit("vector", lambda e: e.scalar_tensor_tensor(
        cmax16[:, :], cmax16[:, :], 1.0, stats_rinv[:, :],
        ALU.mult, ALU.mult))
    sc.slow("vector", lambda e: e.reduce_max(
        pmax1[:, :], cmax16[:, :], axis=AX.X))
    _v = sc.now("v")
    sc.wait("gpsimd", "v", _v)
    b1 = sc.dma("gpsimd", lambda e: e.dma_start(out=red_d, in_=pmax1[:, :]))
    sc.wait("gpsimd", "sg", b1)
    b2 = sc.dma("gpsimd", lambda e: e.dma_start(
        out=pm_row[0:1, :], in_=red_d.rearrange("p one -> one p")))
    sc.wait("vector", "sg", b2)
    sc.slow("vector", lambda e: e.reduce_max(
        gm1[0:1, 0:1], pm_row[0:1, :], axis=AX.X))
    _v = sc.now("v")
    sc.wait("gpsimd", "v", _v)
    b3 = sc.dma("gpsimd", lambda e: e.dma_start(out=cc1i.ap(), in_=gm1[0:1, 0:1]))
    sc.wait("gpsimd", "sg", b3)
    sc.emit("gpsimd", lambda e: e.collective_compute(
        "AllReduce", ALU.max, replica_groups=[list(range(NCORES))],
        ins=[cc1i.ap().opt()], outs=[cc1o.ap().opt()]), sem="cc")
    sc.wait("gpsimd", "cc", sc.now("cc"))
    b4 = sc.dma("gpsimd", lambda e: e.dma_start(
        out=gmx_col[:, :], in_=cc1o.ap().broadcast_to([128, 1])))

    sc.wait("scalar", "si", sc.now("si"))
    sc.emit("scalar", lambda e: e.activation(
        sg_col[:, :], clipx_col[:, :], AF.Sigmoid), sem="s")
    sc.wait("vector", "s", sc.now("s"))
    sc.wait("vector", "sg", b4)
    sc.emit("vector", lambda e: e.scalar_tensor_tensor(
        sx_col[:, :], gmx_col[:, :], 1.0 / QMAX, sg_col[:, :],
        ALU.mult, ALU.mult), sem="v")
    sc.slow("vector", lambda e: e.reciprocal(
        inv_sx_col[:, :], sx_col[:, :]))

    sc.emit("vector", lambda e: e.tensor_scalar(
        qcol[:, :], stats_rinv[:, :], inv_sx_col[:, :], None, ALU.mult), sem="v")
    sc.wait_now("vector", "v")
    xq_tp_arr = [0] * NT
    for j in range(NT):
        sc.emit("vector", lambda e, j=j: e.tensor_scalar(
            q0t[:, :], xc_full[:, j * D:(j + 1) * D], qcol[:, j:j + 1],
            MAGIC, ALU.mult, ALU.add))
        sc.emit("vector", lambda e: e.tensor_scalar(
            q0t[:, :], q0t[:, :], -MAGIC, -QMAX - 1.0, ALU.add, ALU.max))
        if j >= 2:
            sc.wait("vector", "si", xq_tp_arr[j - 2])
        sc.emit("vector", lambda e, j=j: e.tensor_scalar(
            qbt[j % 2][:, :], q0t[:, :], QMAX, None, ALU.min), sem="v")
        _v = sc.now("v")
        sc.wait("sync", "v", _v)
        xq_tp_arr[j] = sc.dma("sync", lambda e, j=j: e.dma_start_transpose(
            xq_fm[:, :].rearrange("p (c t) -> p c t", c=KD)[:, :, j * 128:(j + 1) * 128],
            qbt[j % 2][:, :].rearrange("t (c d) -> t c d", c=KD)))

    sc.wait("gpsimd", "si", xq_tp_arr[NT - 1])
    sc.dma("gpsimd", lambda e: e.dma_start(out=xq_dram, in_=xq_fm[:, :]))

    sc.barrier()
    p0.free()

    pooly = Pool(nc, side="right")
    y_hi = pooly.sb("y_hi", [128, KD * T], BF16)
    y_lo = pooly.sb("y_lo", [128, KD * T], BF16)
    p1 = Pool(nc, side="right")
    wblk = [p1.sb(f"wblk{i}", [128, KD * 128]) for i in range(2)]
    wh = [[p1.sb(f"wh{g}{i}", [128, KD * 128], BF16) for i in range(2)] for g in range(3)]
    wl = [[p1.sb(f"wl{g}{i}", [128, KD * 128], BF16) for i in range(2)] for g in range(3)]
    a_c = p1.sb("a_c", [128, T])
    u_c = p1.sb("u_c", [128, T])
    y_c = p1.sb("y_c", [128, T])
    s1_t = [p1.sb("s1_t0", [128, 512])] * 2
    r_t = [p1.sb("r_t0", [128, 512])] * 2
    v_t = [p1.sb("v_t0", [128, 512])] * 2
    t1_t = [p1.sb("t1_t0", [128, 512])] * 2
    pg = [[p1.ps(f"pg{g}{i}", [128, 512]) for i in range(2)] for g in range(3)]

    win_r = win_d

    wsplit_done = {}
    wsplit_lin = []
    evac_done = {}
    scan_done = [0] * KD
    p1_last_grp = {}

    def p1_split(c):
        for g in range(3):
            eb = g * KD + c
            b = c * 3 + g
            wb = b % 2
            if b >= 2:
                sc.wait("sync", "v", wsplit_lin[b - 2])
            ld = sc.dma("sync", lambda e, wb=wb, eb=eb: e.dma_start(
                out=wblk[wb][:, :].rearrange("p (k e) -> p k e", k=KD),
                in_=win_r[eb]))
            sc.wait("gpsimd", "si", ld)
            if c >= 2:
                sc.wait("gpsimd", "t", p1_last_grp[c - 2])
            cast = sc.dma("gpsimd", lambda e, g=g, c=c, wb=wb: e.dma_start(
                out=wh[g][c % 2][:, :], in_=wblk[wb][:, :]))
            sc.wait("vector", "sg", cast)
            if c >= 2:
                sc.wait("vector", "t", p1_last_grp[c - 2])
            sc.emit("vector", lambda e, g=g, c=c, wb=wb: e.tensor_tensor(
                wl[g][c % 2][:, :], wblk[wb][:, :], wh[g][c % 2][:, :],
                ALU.subtract), sem="v")
            wsplit_done[(c, g)] = sc.now("v")
            wsplit_lin.append(sc.now("v"))

    p1_split(0)
    for c in range(KD):
        for t in range(T4):
            for g in range(3):
                sc.wait("tensor", "v", wsplit_done[(c, g)])
                if c == 0 and t == 0 and g == 0:
                    sc.wait("tensor", "si", xr_tp_done)
                key = (c, t - 2) if t >= 2 else (c - 1, t + 2)
                if key in evac_done:
                    sc.wait("tensor", "s", evac_done[key][0])
                    sc.wait("tensor", "v", evac_done[key][1])
                for pi, (lhs, rhs) in enumerate(
                        [(wh, xrh_fm), (wh, xrl_fm), (wl, xrh_fm)]):
                    for k in range(KD):
                        sc.emit("tensor", lambda e, g=g, c=c, t=t, k=k, pi=pi,
                                lhs=lhs, rhs=rhs: e.matmul(
                            pg[g][t % 2][:, :],
                            lhs[g][c % 2][:, k * 128:(k + 1) * 128],
                            rhs[:, k * T + t * 512:k * T + (t + 1) * 512],
                            start=(pi == 0 and k == 0),
                            stop=(pi == 2 and k == KD - 1)),
                            sem=("t" if (pi == 2 and k == KD - 1) else None))
            grp = sc.now("t")
            sc.wait("scalar", "t", grp)
            if t >= 1:
                sc.wait("scalar", "v", evac_done[(c, t - 1)][1])
            if t == 0 and c >= 1:
                sc.wait("scalar", "v", scan_done[c - 1])
            sl = (t * 512, (t + 1) * 512)
            sc.emit("scalar", lambda e, c=c, t=t, sl=sl: e.activation(
                a_c[:, sl[0]:sl[1]], pg[0][t % 2][:, :], AF.Sigmoid))
            sc.emit("scalar", lambda e, c=c, t=t: e.activation(
                s1_t[t % 2][:, :], pg[0][t % 2][:, :], AF.Sigmoid, scale=-1.0))
            sc.emit("scalar", lambda e, c=c, t=t: e.activation(
                r_t[t % 2][:, :], pg[1][t % 2][:, :], AF.Sigmoid))
            sc.emit("scalar", lambda e, c=c, t=t: e.activation(
                v_t[t % 2][:, :], pg[2][t % 2][:, :], AF.Gelu), sem="s")
            s_ev = sc.now("s")
            sc.wait("vector", "s", s_ev)
            sc.emit("vector", lambda e, t=t: e.tensor_tensor(
                t1_t[t % 2][:, :], r_t[t % 2][:, :], v_t[t % 2][:, :], ALU.mult))
            sc.emit("vector", lambda e, t=t, sl=sl: e.tensor_tensor(
                u_c[:, sl[0]:sl[1]], t1_t[t % 2][:, :], s1_t[t % 2][:, :],
                ALU.mult), sem="v")
            evac_done[(c, t)] = (s_ev, sc.now("v"))
            if t == 0 and c + 1 < KD:
                p1_split(c + 1)
        p1_last_grp[c] = sc.now("t")
        sc.slow("vector", lambda e, c=c: e.tensor_tensor_scan(
            y_c[:, :], a_c[:, :], u_c[:, :], 0.0, ALU.mult, ALU.add))
        sc.emit("vector", lambda e, c=c: e.tensor_copy(
            y_hi[:, c * T:(c + 1) * T], y_c[:, :]))
        sc.emit("vector", lambda e, c=c: e.tensor_tensor(
            y_lo[:, c * T:(c + 1) * T], y_c[:, :], y_hi[:, c * T:(c + 1) * T],
            ALU.subtract), sem="v")
        scan_done[c] = sc.now("v")

    sc.barrier()
    p1.free()
    poolA.free()

    poolg = Pool(nc)
    qwgT = poolg.sb("qwgT", [128, KD * I_], BF16)
    p2a = Pool(nc, side="right")
    woh = p2a.sb("woh", [128, KD * D], BF16)
    wol = p2a.sb("wol", [128, KD * D], BF16)
    attn_sc = [p2a.sb("attn_sc0", [128, D])] * 2
    pw = [p2a.ps(f"pw{i}", [128, D]) for i in range(2)]
    wg_st = [p2a.sb(f"wg_st{i}", [128, D]) for i in range(2)]
    wq0 = p2a.sb("wq0", [128, D])
    wqb = [p2a.sb(f"wqb{i}", [128, D], BF16) for i in range(2)]
    rmaxc = p2a.sb("rmaxc", [128, 2])
    p2w = Pool(nc, side="right")
    wo_st = p2w.sb("wo_st", [128, KD * D])

    sc.wait("scalar", "si", sc.now("si"))
    sc.emit("scalar", lambda e: e.activation(
        lam_col[:, :], lam_col[:, :], AF.Sigmoid), sem="s")
    wor = wout_d.rearrange("(k p) d -> p k d", p=128)
    sc.wait("vector", "s", sc.now("s"))
    for k in range(KD):
        ldk = sc.dma("sync", lambda e, k=k: e.dma_start(
            out=wo_st[:, k * D:(k + 1) * D], in_=wor[:, k]))
        sc.wait("vector", "si", ldk)
        sc.emit("vector", lambda e, k=k: e.tensor_scalar(
            wo_st[:, k * D:(k + 1) * D], wo_st[:, k * D:(k + 1) * D],
            lam_col[:, k:k + 1], None, ALU.mult), sem="v")
        _v = sc.now("v")
        sc.wait("gpsimd", "v", _v)
        wck = sc.dma("gpsimd", lambda e, k=k: e.dma_start(
            out=woh[:, k * D:(k + 1) * D], in_=wo_st[:, k * D:(k + 1) * D]))
        sc.wait("vector", "sg", wck)
        sc.emit("vector", lambda e, k=k: e.tensor_tensor(
            wol[:, k * D:(k + 1) * D], wo_st[:, k * D:(k + 1) * D],
            woh[:, k * D:(k + 1) * D], ALU.subtract), sem="v")
    wo_split = sc.now("v")
    p2w.free()

    p2d = Pool(nc, side="right")
    wd_st2 = [p2d.sb(f"wd_st2{i}", [128, I_]) for i in range(2)]

    wout_evac = {}
    qwg_tp = [0] * KI

    def p2_wout_step(tc):
        sc.wait("tensor", "v", wo_split)
        if tc >= 2:
            sc.wait("tensor", "v", wout_evac[tc - 2])
        n = 0
        for nh in range(2):
            for pi, (lhs, rhs) in enumerate([(y_hi, woh), (y_hi, wol), (y_lo, woh)]):
                for k in range(KD):
                    n += 1
                    sc.emit("tensor", lambda e, tc=tc, nh=nh, k=k, pi=pi,
                            lhs=lhs, rhs=rhs: e.matmul(
                        pw[tc % 2][:, nh * 512:(nh + 1) * 512],
                        lhs[:, k * T + tc * 128:k * T + (tc + 1) * 128],
                        rhs[:, k * D + nh * 512:k * D + (nh + 1) * 512],
                        start=(pi == 0 and k == 0),
                        stop=(pi == 2 and k == KD - 1)),
                        sem=("t" if (pi == 2 and k == KD - 1) else None))
        grp = sc.now("t")
        sc.wait("vector", "t", grp)
        if tc >= 1:
            sc.wait("vector", "sg", asp_done[tc - 1])
        sc.emit("vector", lambda e, tc=tc: e.tensor_copy(
            attn_sc[tc % 2][:, :], pw[tc % 2][:, :]), sem="v")
        wout_evac[tc] = sc.now("v")
        sc.wait("gpsimd", "v", wout_evac[tc])
        asp_done[tc] = sc.dma("gpsimd", lambda e, tc=tc: e.dma_start(
            out=asp_d[tc], in_=attn_sc[tc % 2][:, :]))

    wgq_v = {}

    def p2_wgq_step(ic):
        if ic >= 2:
            sc.wait("sync", "v", wgq_v[ic - 2])
        ld = sc.dma("sync", lambda e, ic=ic: e.dma_start(
            out=wg_st[ic % 2][:, :], in_=wg_d[ic * 128:(ic + 1) * 128, :]))
        sc.wait("vector", "si", ld)
        sc.slow("vector", lambda e, ic=ic: e.reduce_max(
            rmaxc[:, ic % 2:ic % 2 + 1], wg_st[ic % 2][:, :], axis=AX.X,
            apply_absolute_value=True))
        sc.emit("vector", lambda e, ic=ic: e.tensor_scalar(
            comb_g[:, ic:ic + 1], rmaxc[:, ic % 2:ic % 2 + 1],
            sx_col[:, :], 1.0 / QMAX, ALU.mult, ALU.mult))
        sc.slow("vector", lambda e, ic=ic: e.reciprocal(
            rmaxc[:, ic % 2:ic % 2 + 1], rmaxc[:, ic % 2:ic % 2 + 1]))
        sc.emit("vector", lambda e, ic=ic: e.tensor_scalar(
            wq0[:, :], wg_st[ic % 2][:, :], rmaxc[:, ic % 2:ic % 2 + 1],
            QMAX, ALU.mult, ALU.mult))
        if ic >= 2:
            sc.wait("vector", "si", qwg_tp[ic - 2])
        sc.emit("vector", lambda e, ic=ic: e.tensor_scalar(
            wqb[ic % 2][:, :], wq0[:, :], MAGIC, -MAGIC, ALU.add, ALU.add),
            sem="v")
        wgq_v[ic] = sc.now("v")
        sc.wait("sync", "v", wgq_v[ic])
        qwg_tp[ic] = sc.dma("sync", lambda e, ic=ic: e.dma_start_transpose(
            qwgT[:, :].rearrange("p (k i) -> p k i", k=KD)[:, :, ic * 128:(ic + 1) * 128],
            wqb[ic % 2][:, :].rearrange("i (k h) -> i k h", k=KD)))

    wdred_v = [0] * KD

    def p2_wdred_step(oc):
        if oc >= 2:
            sc.wait("sync", "v", wdred_v[oc - 2])
        ldw = sc.dma("sync", lambda e, oc=oc: e.dma_start(
            out=wd_st2[oc % 2][:, :], in_=wd_d[oc * 128:(oc + 1) * 128, :]))
        sc.wait("vector", "si", ldw)
        sc.slow("vector", lambda e, oc=oc: e.reduce_max(
            drmax8_p[:, oc:oc + 1], wd_st2[oc % 2][:, :], axis=AX.X,
            apply_absolute_value=True))
        sc.emit("vector", lambda e, oc=oc: e.tensor_scalar(
            comb_d[:, oc:oc + 1], drmax8_p[:, oc:oc + 1], 1.0, None, ALU.mult))
        sc.slow("vector", lambda e, oc=oc: e.reciprocal(
            drmax8_p[:, oc:oc + 1], drmax8_p[:, oc:oc + 1]))
        wdred_v[oc] = sc.now("v")

    asp_done = {}
    steps = []
    for i in range(max(NT, KI)):
        if i < NT:
            steps.append(("w", i))
        if i < KI:
            steps.append(("q", i))
        if 14 <= i < 14 + KD:
            steps.append(("d", i - 14))
    for kind, i in steps:
        if kind == "w":
            p2_wout_step(i)
        elif kind == "q":
            p2_wgq_step(i)
        else:
            p2_wdred_step(i)

    sc.barrier()
    p2d.free()
    p2a.free()
    pooly.free()

    poolxq2 = Pool(nc)
    xq_fm2 = poolxq2.sb("xq_fm2", [128, KD * T], BF16)
    xq_ld = sc.dma("gpsimd", lambda e: e.dma_start(out=xq_fm2[:, :], in_=xq_dram))
    p3 = Pool(nc, side="right")
    qwuT = p3.sb("qwuT", [128, KD * I_], BF16)
    wu_st = [p3.sb(f"wu_st{i}", [128, D]) for i in range(2)]
    uq0 = p3.sb("uq0", [128, D])
    uqb = [p3.sb(f"uqb{i}", [128, D], BF16) for i in range(2)]
    urmax = p3.sb("urmax", [128, 2])
    sg_t = [p3.sb(f"sg_t{i}", [128, 512]) for i in range(2)]
    uu_t = [p3.sb(f"uu_t{i}", [128, 512]) for i in range(2)]
    h_ch = [p3.sb(f"h_ch{i}", [128, T]) for i in range(2)]
    hrun = p3.sb("hrun", [128, 1])
    habs = p3.sb("habs", [128, 1])
    pgu = [[p3.ps(f"pgu{a}{i}", [128, 512]) for i in range(2)] for a in range(2)]

    sc.emit("vector", lambda e: e.memset(hrun[:, :], 0.0))

    qwu_tp = [0] * KI
    uq_v = {}

    def p3_wuq_step(ic):
        if ic >= 2:
            sc.wait("sync", "v", uq_v[ic - 2])
        ld = sc.dma("sync", lambda e, ic=ic: e.dma_start(
            out=wu_st[ic % 2][:, :], in_=wu_d[ic * 128:(ic + 1) * 128, :]))
        sc.wait("vector", "si", ld)
        sc.slow("vector", lambda e, ic=ic: e.reduce_max(
            urmax[:, ic % 2:ic % 2 + 1], wu_st[ic % 2][:, :], axis=AX.X,
            apply_absolute_value=True))
        sc.emit("vector", lambda e, ic=ic: e.tensor_scalar(
            comb_u[:, ic:ic + 1], urmax[:, ic % 2:ic % 2 + 1],
            sx_col[:, :], 1.0 / QMAX, ALU.mult, ALU.mult))
        sc.slow("vector", lambda e, ic=ic: e.reciprocal(
            urmax[:, ic % 2:ic % 2 + 1], urmax[:, ic % 2:ic % 2 + 1]))
        sc.emit("vector", lambda e, ic=ic: e.tensor_scalar(
            uq0[:, :], wu_st[ic % 2][:, :], urmax[:, ic % 2:ic % 2 + 1],
            QMAX, ALU.mult, ALU.mult))
        if ic >= 2:
            sc.wait("vector", "si", qwu_tp[ic - 2])
        sc.emit("vector", lambda e, ic=ic: e.tensor_scalar(
            uqb[ic % 2][:, :], uq0[:, :], MAGIC, -MAGIC, ALU.add, ALU.add),
            sem="v")
        uq_v[ic] = sc.now("v")
        sc.wait("sync", "v", uq_v[ic])
        qwu_tp[ic] = sc.dma("sync", lambda e, ic=ic: e.dma_start_transpose(
            qwuT[:, :].rearrange("p (k i) -> p k i", k=KD)[:, :, ic * 128:(ic + 1) * 128],
            uqb[ic % 2][:, :].rearrange("i (k h) -> i k h", k=KD)))

    p3_ev = {}
    p3_tt = {}
    hsp_done = {}
    for ic in range(KI):
        if ic == 0:
            p3_wuq_step(0)
        for t in range(T4):
            sc.wait("tensor", "si", qwu_tp[ic])
            if ic == 0 and t == 0:
                sc.wait("tensor", "sg", xq_ld)
            key = (ic, t - 2) if t >= 2 else (ic - 1, t + 2)
            if key in p3_ev:
                sc.wait("tensor", "s", p3_ev[key])
            for ai, wT in enumerate((qwgT, qwuT)):
                for k in range(KD):
                    sc.emit("tensor", lambda e, ic=ic, t=t, k=k, ai=ai, wT=wT:
                            e.matmul(
                        pgu[ai][t % 2][:, :],
                        wT[:, k * I_ + ic * 128:k * I_ + (ic + 1) * 128],
                        xq_fm2[:, k * T + t * 512:k * T + (t + 1) * 512],
                        start=(k == 0), stop=(k == KD - 1)),
                        sem=("t" if (ai == 1 and k == KD - 1) else None))
            grp = sc.now("t")
            sc.wait("scalar", "t", grp)
            if key in p3_tt:
                sc.wait("scalar", "v", p3_tt[key])
            sc.emit("scalar", lambda e, ic=ic, t=t: e.activation(
                sg_t[t % 2][:, :], pgu[0][t % 2][:, :], AF.Silu,
                scale=comb_g[:, ic:ic + 1]))
            sc.emit("scalar", lambda e, ic=ic, t=t: e.activation(
                uu_t[t % 2][:, :], pgu[1][t % 2][:, :], AF.Identity,
                scale=comb_u[:, ic:ic + 1]), sem="s")
            p3_ev[(ic, t)] = sc.now("s")
            sc.wait("vector", "s", p3_ev[(ic, t)])
            if t == 0 and ic >= 2:
                sc.wait("vector", "sg", hsp_done[ic - 2])
            sc.emit("vector", lambda e, ic=ic, t=t: e.tensor_tensor(
                h_ch[ic % 2][:, t * 512:(t + 1) * 512], sg_t[t % 2][:, :],
                uu_t[t % 2][:, :], ALU.mult), sem="v")
            p3_tt[(ic, t)] = sc.now("v")
            if t == 0 and ic + 1 < KI:
                p3_wuq_step(ic + 1)
        sc.slow("vector", lambda e, ic=ic: e.reduce_max(
            habs[:, :], h_ch[ic % 2][:, :], axis=AX.X,
            apply_absolute_value=True))
        sc.emit("vector", lambda e: e.scalar_tensor_tensor(
            hrun[:, :], habs[:, :], 1.0, hrun[:, :], ALU.mult, ALU.max),
            sem="v")
        _v = sc.now("v")
        sc.wait("gpsimd", "v", _v)
        hsp_done[ic] = sc.dma("gpsimd", lambda e, ic=ic: e.dma_start(
            out=hsp_d.rearrange("g p x -> p g x")[:, :, ic * 256:(ic + 1) * 256],
            in_=h_ch[ic % 2][:, :].rearrange("p (g t) -> p g t", g=T // 256)))

    _v = sc.now("v")
    sc.wait("gpsimd", "v", _v)
    c1 = sc.dma("gpsimd", lambda e: e.dma_start(out=red_d, in_=hrun[:, :]))
    sc.wait("gpsimd", "sg", c1)
    c2 = sc.dma("gpsimd", lambda e: e.dma_start(
        out=pm_row[0:1, :], in_=red_d.rearrange("p one -> one p")))
    sc.wait("vector", "sg", c2)
    sc.slow("vector", lambda e: e.reduce_max(
        gm1[0:1, 0:1], pm_row[0:1, :], axis=AX.X))
    _v = sc.now("v")
    sc.wait("gpsimd", "v", _v)
    c3 = sc.dma("gpsimd", lambda e: e.dma_start(out=cc2i.ap(), in_=gm1[0:1, 0:1]))
    sc.wait("gpsimd", "sg", c3)
    sc.emit("gpsimd", lambda e: e.collective_compute(
        "AllReduce", ALU.max, replica_groups=[list(range(NCORES))],
        ins=[cc2i.ap().opt()], outs=[cc2o.ap().opt()]), sem="cc")
    cc2_done = sc.now("cc")

    sc.barrier()
    p3.free()
    poolxq2.free()
    poolg.free()

    p4 = Pool(nc, side="right")
    qwdT = p4.sb("qwdT", [128, KI * D], BF16)
    wd_st = [p4.sb(f"wd_st{i}", [128, I_]) for i in range(2)]
    dqb = [p4.sb(f"dqb{i}", [128, I_], BF16) for i in range(2)]
    h_g = [p4.sb(f"h_g{i}", [128, KI * 256]) for i in range(2)]
    qh_g = [p4.sb(f"qh_g{i}", [128, KI * 256], BF16) for i in range(2)]
    x_t4 = [p4.sb(f"x_t4{i}", [128, D]) for i in range(2)]
    xc_t4 = p4.sb("xc_t4", [128, D])
    xn_t4 = [p4.sb(f"xn_t4{i}", [128, D]) for i in range(2)]
    at_t4 = [p4.sb(f"at_t4{i}", [128, D]) for i in range(2)]
    o_t4 = [p4.sb(f"o_t4{i}", [128, D]) for i in range(2)]
    pd = [p4.ps(f"pd{i}", [128, D]) for i in range(4)]

    NG = T // 256

    sc.wait("gpsimd", "cc", cc2_done)
    c4 = sc.dma("gpsimd", lambda e: e.dma_start(
        out=gmx_col[:, :], in_=cc2o.ap().broadcast_to([128, 1])))
    sc.emit("scalar", lambda e: e.activation(
        sg_col[:, :], clipd_col[:, :], AF.Sigmoid), sem="s")
    sc.wait("vector", "s", sc.now("s"))
    sc.wait("vector", "sg", c4)
    sc.emit("vector", lambda e: e.scalar_tensor_tensor(
        sh_col[:, :], gmx_col[:, :], 1.0 / QMAX, sg_col[:, :],
        ALU.mult, ALU.mult), sem="v")
    sc.slow("vector", lambda e: e.reciprocal(
        inv_sh_col[:, :], sh_col[:, :]))

    hq_g = [0] * NG
    grp_g = [0] * NG
    hgl_arr = [0] * NG

    def h_load(g):
        if g >= 2:
            sc.wait("scalar", "v", hq_g[g - 2])
        hgl_arr[g] = sc.dma("scalar", lambda e, g=g: e.dma_start(
            out=h_g[g % 2][:, :], in_=hsp_d[g]))

    def h_quant(g):
        sc.wait("vector", "ss", hgl_arr[g])
        if g >= 2:
            sc.wait("vector", "t", grp_g[g - 2])
        sc.emit("vector", lambda e, g=g: e.tensor_scalar(
            h_g[g % 2][:, :], h_g[g % 2][:, :], inv_sh_col[:, :], MAGIC,
            ALU.mult, ALU.add))
        sc.emit("vector", lambda e, g=g: e.tensor_scalar(
            h_g[g % 2][:, :], h_g[g % 2][:, :], -MAGIC, -QMAX - 1.0,
            ALU.add, ALU.max))
        sc.emit("vector", lambda e, g=g: e.tensor_scalar(
            qh_g[g % 2][:, :], h_g[g % 2][:, :], QMAX, None, ALU.min), sem="v")
        hq_g[g] = sc.now("v")

    h_load(0)
    h_load(1)

    sc.emit("vector", lambda e: e.tensor_scalar(
        comb_d[:, :], comb_d[:, :], sh_col[:, :], 1.0 / QMAX,
        ALU.mult, ALU.mult), sem="v")

    wdq_v = {}
    wd_tp = [0] * KD
    for oc in range(KD):
        if oc >= 2:
            sc.wait("sync", "v", wdq_v[oc - 2])
        ld = sc.dma("sync", lambda e, oc=oc: e.dma_start(
            out=wd_st[oc % 2][:, :], in_=wd_d[oc * 128:(oc + 1) * 128, :]))
        sc.wait("vector", "si", ld)
        sc.emit("vector", lambda e, oc=oc: e.tensor_scalar(
            wd_st[oc % 2][:, :], wd_st[oc % 2][:, :], drmax8_p[:, oc:oc + 1],
            QMAX, ALU.mult, ALU.mult))
        if oc >= 2:
            sc.wait("vector", "ss", wd_tp[oc - 2])
        sc.emit("vector", lambda e, oc=oc: e.tensor_scalar(
            dqb[oc % 2][:, :], wd_st[oc % 2][:, :], MAGIC, -MAGIC,
            ALU.add, ALU.add), sem="v")
        wdq_v[oc] = sc.now("v")
        sc.wait("scalar", "v", wdq_v[oc])
        wd_tp[oc] = sc.dma("scalar", lambda e, oc=oc: e.dma_start_transpose(
            qwdT[:, :].rearrange("p (k o) -> p k o", k=KI)[:, :, oc * 128:(oc + 1) * 128],
            dqb[oc % 2][:, :].rearrange("o (k i) -> o k i", k=KI)))


    _v = sc.now("v")
    sc.wait("gpsimd", "v", _v)
    d1b = sc.dma("gpsimd", lambda e: e.dma_start(
        out=os_d.rearrange("c p -> p c"), in_=comb_d[:, :],
        allow_slow_non_contiguous=True))
    sc.wait("gpsimd", "sg", d1b)
    d2b = sc.dma("gpsimd", lambda e: e.dma_start(
        out=oscale_bc[:, :],
        in_=os_d.rearrange("c p -> (c p)").rearrange("(a d) -> a d", a=1)
        .broadcast_to([128, D])))
    sc.wait("vector", "sg", d2b)

    h_quant(0)
    p4_ev = {}
    out_done = {}
    for g in range(NG):
        if g + 1 < NG:
            h_load(g + 1)
        for s in range(2):
            tc = g * 2 + s
            if tc >= 2:
                sc.wait("sync", "v", p4_ev[tc - 2][1])
            xld = sc.dma("sync", lambda e, tc=tc: e.dma_start(
                out=x_t4[tc % 2][:, :], in_=x_d[tc * 128:(tc + 1) * 128, :]))
            if tc >= 2:
                sc.wait("gpsimd", "v", p4_ev[tc - 2][1])
            ald = sc.dma("gpsimd", lambda e, tc=tc: e.dma_start(
                out=at_t4[tc % 2][:, :], in_=asp_d[tc]))
            sc.wait("tensor", "v", hq_g[g])
            if tc == 0:
                sc.wait("tensor", "ss", wd_tp[KD - 1])
            if tc >= 4:
                sc.wait("tensor", "v", p4_ev[tc - 4][1])
            for nh in range(2):
                for k in range(KI):
                    sc.emit("tensor", lambda e, s=s, g=g, tc=tc, nh=nh, k=k: e.matmul(
                        pd[tc % 4][:, nh * 512:(nh + 1) * 512],
                        qh_g[g % 2][:, k * 256 + s * 128:k * 256 + (s + 1) * 128],
                        qwdT[:, k * D + nh * 512:k * D + (nh + 1) * 512],
                        start=(k == 0), stop=(k == KI - 1)),
                        sem=("t" if (nh == 1 and k == KI - 1) else None))
            grp = sc.now("t")
            grp_g[g] = grp
            if s == 0 and g + 1 < NG:
                h_quant(g + 1)
            sc.wait("scalar", "si", xld)
            sc.emit("scalar", lambda e, tc=tc: e.activation(
                xc_t4[:, :], x_t4[tc % 2][:, :], AF.Identity,
                bias=stats_negm[:, tc:tc + 1]))
            sc.emit("scalar", lambda e, tc=tc: e.activation(
                xn_t4[tc % 2][:, :], xc_t4[:, :], AF.Identity,
                scale=stats_rinv[:, tc:tc + 1]), sem="s")
            sres = sc.now("s")
            sc.wait("vector", "t", grp)
            sc.wait("vector", "s", sres)
            sc.wait("vector", "sg", ald)
            if tc >= 2:
                sc.wait("vector", "si", out_done[tc - 2])
            sc.emit("vector", lambda e, tc=tc: e.tensor_tensor(
                xn_t4[tc % 2][:, :], xn_t4[tc % 2][:, :], x_t4[tc % 2][:, :], ALU.add))
            sc.emit("vector", lambda e, tc=tc: e.tensor_tensor(
                o_t4[tc % 2][:, :], pd[tc % 4][:, :], oscale_bc[:, :], ALU.mult))
            sc.emit("vector", lambda e, tc=tc: e.tensor_tensor(
                o_t4[tc % 2][:, :], o_t4[tc % 2][:, :], at_t4[tc % 2][:, :], ALU.add))
            sc.emit("vector", lambda e, tc=tc: e.tensor_tensor(
                o_t4[tc % 2][:, :], o_t4[tc % 2][:, :], xn_t4[tc % 2][:, :], ALU.add),
                sem="v")
            p4_ev[tc] = (grp, sc.now("v"))
            sc.wait("sync", "v", p4_ev[tc][1])
            out_done[tc] = sc.dma("sync", lambda e, tc=tc: e.dma_start(
                out=out_d[tc * 128:(tc + 1) * 128, :], in_=o_t4[tc % 2][:, :]))

    sc.wait_now("sync", "si")
    sc.wait_now("gpsimd", "sg")

    semnames = ["si", "sg", "ss", "v", "s", "t", "cc"]
    for n in semnames:
        g = nc.semaphore(n)
        sc.semh[n] = g.__enter__()
    with nc.Block() as block:
        @block.sync
        def _(eng):
            for f in sc.ops["sync"]:
                f(eng)

        @block.vector
        def _(eng):
            for f in sc.ops["vector"]:
                f(eng)

        @block.scalar
        def _(eng):
            for f in sc.ops["scalar"]:
                f(eng)

        @block.tensor
        def _(eng):
            for f in sc.ops["tensor"]:
                f(eng)

        @block.gpsimd
        def _(eng):
            for f in sc.ops["gpsimd"]:
                f(eng)
    perm.free()
    p4.free()
    return nc


def kernel(**inputs):
    f32 = lambda k: np.ascontiguousarray(np.asarray(inputs[k], dtype=np.float32))
    x = f32("L0_attn_x")
    shared = {
        "L0_attn_w_norm": f32("L0_attn_w_norm"),
        "L0_attn_W_in": np.ascontiguousarray(
            f32("L0_attn_W_in").reshape(KD, 128, KE, 128).transpose(2, 1, 0, 3)),
        "L0_attn_lambda_log": f32("L0_attn_lambda_log"),
        "L0_attn_W_out": f32("L0_attn_W_out"),
        "L0_ffn_clip_x": f32("L0_ffn_clip_x"),
        "L0_ffn_clip_down": f32("L0_ffn_clip_down"),
        "L0_ffn_w_gate": f32("L0_ffn_w_gate"),
        "L0_ffn_w_up": f32("L0_ffn_w_up"),
        "L0_ffn_w_down": f32("L0_ffn_w_down"),
    }
    nc = build()
    in_maps = [dict(shared, L0_attn_x=np.ascontiguousarray(x[c]))
               for c in range(NCORES)]
    res = run_bass_kernel_spmd(nc, in_maps, core_ids=list(range(NCORES)))
    return np.stack(
        [np.asarray(r["out"], dtype=np.float32) for r in res.results], axis=0)


# version 29
# speedup vs baseline: 1.0398x; 1.0398x over previous
import numpy as np
import concourse.bass as bass
import concourse.mybir as mybir
from concourse.bass_utils import run_bass_kernel_spmd

F32 = mybir.dt.float32
BF16 = mybir.dt.bfloat16
I32 = mybir.dt.int32
AF = mybir.ActivationFunctionType
ALU = mybir.AluOpType
AX = mybir.AxisListType

MAGIC = 12582912.0
T = 2048
D = 1024
E = 3072
I_ = 2816
KD = D // 128
KE = E // 128
KI = I_ // 128
NT = T // 128
T4 = T // 512
NCORES = 8
QMAX = 127.0


class Sched:

    ENGS = ("sync", "vector", "scalar", "tensor", "gpsimd")

    def __init__(self):
        self.ops = {e: [] for e in self.ENGS}
        self.cnt = {}
        self.semh = {}

    def inc(self, sem, by):
        self.cnt[sem] = self.cnt.get(sem, 0) + by
        return self.cnt[sem]

    def now(self, sem):
        return self.cnt.get(sem, 0)

    def emit(self, eng, mk, sem=None, by=1):
        if sem is None:
            self.ops[eng].append(lambda e, mk=mk: mk(e))
        else:
            self.inc(sem, by)
            self.ops[eng].append(
                lambda e, mk=mk, s=sem, b=by: mk(e).then_inc(self.semh[s], b))

    def dma(self, eng, mk):
        sem = {"sync": "si", "scalar": "ss"}.get(eng, "sg")
        self.emit(eng, mk, sem=sem, by=16)
        return self.now(sem)

    def wait(self, eng, sem, val):
        if val <= 0:
            return
        self.ops[eng].append(
            lambda e, s=sem, v=val: e.wait_ge(self.semh[s], v))

    def wait_now(self, eng, sem):
        self.wait(eng, sem, self.now(sem))

    def slow(self, eng, mk, sem="v"):
        self.emit(eng, lambda e: e.engine_nop(), sem=sem)
        self.wait_now(eng, sem)
        self.emit(eng, mk, sem=sem)
        self.wait_now(eng, sem)

    def barrier(self, engines=None, sems=("si", "sg", "ss", "v", "s", "t")):
        engines = engines or self.ENGS
        vals = {s: self.now(s) for s in sems}
        for e in engines:
            for s, v in vals.items():
                self.wait(e, s, v)


class Pool:
    def __init__(self, nc, side="left"):
        self.nc = nc
        self.side = side
        self.guards = []

    def sb(self, name, shape, dt=F32):
        g = self.nc.sbuf_tensor(name, shape, dt, side=self.side)
        h = g.__enter__()
        self.guards.append(g)
        return h

    def ps(self, name, shape):
        g = self.nc.psum_tensor(name, shape, F32)
        h = g.__enter__()
        self.guards.append(g)
        return h

    def free(self):
        for g in reversed(self.guards):
            g.__exit__(None, None, None)
        self.guards = []


def build():
    nc = bass.Bass("TRN2", target_bir_lowering=False, detect_race_conditions=False)

    x_d = nc.dram_tensor("L0_attn_x", [T, D], F32, kind="ExternalInput").ap()
    wnorm_d = nc.dram_tensor("L0_attn_w_norm", [D], F32, kind="ExternalInput").ap()
    win_d = nc.dram_tensor("L0_attn_W_in", [KE, 128, KD, 128], F32, kind="ExternalInput").ap()
    lam_d = nc.dram_tensor("L0_attn_lambda_log", [D], F32, kind="ExternalInput").ap()
    wout_d = nc.dram_tensor("L0_attn_W_out", [D, D], F32, kind="ExternalInput").ap()
    clipx_d = nc.dram_tensor("L0_ffn_clip_x", [1], F32, kind="ExternalInput").ap()
    clipd_d = nc.dram_tensor("L0_ffn_clip_down", [1], F32, kind="ExternalInput").ap()
    wg_d = nc.dram_tensor("L0_ffn_w_gate", [I_, D], F32, kind="ExternalInput").ap()
    wu_d = nc.dram_tensor("L0_ffn_w_up", [I_, D], F32, kind="ExternalInput").ap()
    wd_d = nc.dram_tensor("L0_ffn_w_down", [D, I_], F32, kind="ExternalInput").ap()
    out_d = nc.dram_tensor("out", [T, D], F32, kind="ExternalOutput").ap()

    hsp_d = nc.dram_tensor("h_spill", [T // 256, 128, KI * 256], F32).ap()
    asp_d = nc.dram_tensor("attn_spill", [NT, 128, D], F32).ap()
    qwdT_d = nc.dram_tensor("qwdT_spill", [128, KI * D], BF16).ap()
    red_d = nc.dram_tensor("red_bounce", [128, 1], F32).ap()
    os_d = nc.dram_tensor("oscale_bounce", [KD, 128], F32).ap()
    cc1i = nc.dram_tensor("cc1_in", [1, 1], F32)
    cc1o = nc.dram_tensor("cc1_out", [1, 1], F32)
    cc2i = nc.dram_tensor("cc2_in", [1, 1], F32)
    cc2o = nc.dram_tensor("cc2_out", [1, 1], F32)

    sc = Sched()

    perm = Pool(nc)
    stats_negm = perm.sb("stats_negm", [128, NT])
    stats_rinv = perm.sb("stats_rinv", [128, NT])
    stats_r3 = perm.sb("stats_r3", [128, NT])
    cmax16 = perm.sb("cmax16", [128, NT])
    wn_bc = perm.sb("wn_bc", [128, D])
    oscale_bc = perm.sb("oscale_bc", [128, D])
    lam_col = perm.sb("lam_col", [128, KD])
    comb_g = perm.sb("comb_g", [128, KI])
    comb_u = perm.sb("comb_u", [128, KI])
    comb_d = perm.sb("comb_d", [128, KD])
    drmax8_p = perm.sb("drmax8_p", [128, KD])
    rinv127_p = perm.sb("rinv127_p", [128, KD])
    inv_sx_col = perm.sb("inv_sx_col", [128, 1])
    sx_col = perm.sb("sx_col", [128, 1])
    inv_sh_col = perm.sb("inv_sh_col", [128, 1])
    sh_col = perm.sb("sh_col", [128, 1])
    gm1 = perm.sb("gm1", [1, 1])
    gmx_col = perm.sb("gmx_col", [128, 1])
    sg_col = perm.sb("sg_col", [128, 1])
    clipx_col = perm.sb("clipx_col", [128, 1])
    clipd_col = perm.sb("clipd_col", [128, 1])
    pm_row = perm.sb("pm_row", [1, 128])

    xq_dram = nc.dram_tensor("xq_spill", [128, KD * T], BF16).ap()

    poolA = Pool(nc)
    xrh_fm = poolA.sb("xrh_fm", [128, KD * T], BF16)
    xrl_fm = poolA.sb("xrl_fm", [128, KD * T], BF16)

    p0 = Pool(nc, side="right")
    xq_fm = p0.sb("xq_fm", [128, KD * T], BF16)
    xc_full = p0.sb("xc_full", [128, NT * D])
    xst = [p0.sb(f"xst{i}", [128, D]) for i in range(2)]
    ssq = p0.sb("ssq", [128, NT])
    sumc = p0.sb("sumc", [128, 2])
    st16a = p0.sb("st16a", [128, NT])
    st16b = p0.sb("st16b", [128, NT])
    st16c = p0.sb("st16c", [128, NT])
    st16d = p0.sb("st16d", [128, NT])
    xrw = [p0.sb(f"xrw{i}", [128, D]) for i in range(2)]
    xrh_t = [p0.sb(f"xrh_t{i}", [128, D], BF16) for i in range(2)]
    xrl_t = [p0.sb(f"xrl_t{i}", [128, D], BF16) for i in range(2)]
    sqscr = xrw[0]
    q0t = xst[0]
    qbt = [p0.sb(f"qbt{i}", [128, D], BF16) for i in range(2)]
    qcol = p0.sb("qcol", [128, NT])
    pmax1 = p0.sb("pmax1", [128, 1])

    sc.dma("sync", lambda e: e.dma_start(
        out=wn_bc[:, :], in_=wnorm_d.rearrange("(a d) -> a d", a=1).broadcast_to([128, D])))
    sc.dma("sync", lambda e: e.dma_start(
        out=clipx_col[:, :],
        in_=clipx_d.rearrange("(a b) -> a b", a=1).broadcast_to([128, 1])))
    sc.dma("sync", lambda e: e.dma_start(
        out=clipd_col[:, :],
        in_=clipd_d.rearrange("(a b) -> a b", a=1).broadcast_to([128, 1])))
    sc.dma("sync", lambda e: e.dma_start(
        out=lam_col[:, :], in_=lam_d.rearrange("(c p) -> p c", p=128),
        allow_slow_non_contiguous=True))

    vdone_p0 = [0] * NT
    sdone_p0 = [0] * NT
    for j in range(NT):
        if j >= 2:
            sc.wait("sync", "s", sdone_p0[j - 2])
        ld = sc.dma("sync", lambda e, j=j: e.dma_start(
            out=xst[j % 2][:, :], in_=x_d[j * 128:(j + 1) * 128, :]))
        sc.wait("vector", "si", ld)
        sc.slow("vector", lambda e, j=j: e.reduce_sum(
            sumc[:, j % 2:j % 2 + 1], xst[j % 2][:, :], axis=AX.X))
        sc.emit("vector", lambda e, j=j: e.tensor_single_scalar(
            stats_negm[:, j:j + 1], sumc[:, j % 2:j % 2 + 1], -1.0 / D, ALU.mult),
            sem="v")
        vdone_p0[j] = sc.now("v")
        sc.wait("scalar", "v", vdone_p0[j])
        sc.emit("scalar", lambda e, j=j: e.activation(
            xc_full[:, j * D:(j + 1) * D], xst[j % 2][:, :], AF.Identity,
            bias=stats_negm[:, j:j + 1]))
        sc.emit("scalar", lambda e, j=j: e.activation(
            sqscr[:, :], xc_full[:, j * D:(j + 1) * D], AF.Square,
            accum_out=ssq[:, j:j + 1]), sem="s")
        sdone_p0[j] = sc.now("s")
        sc.wait("vector", "s", sdone_p0[j])
        sc.slow("vector", lambda e, j=j: e.reduce_max(
            cmax16[:, j:j + 1], xc_full[:, j * D:(j + 1) * D], axis=AX.X,
            apply_absolute_value=True))

    sc.wait_now("vector", "s")
    sc.emit("vector", lambda e: e.tensor_scalar(
        st16a[:, :], ssq[:, :], 1.0 / D, 1e-5, ALU.mult, ALU.add))
    sc.slow("vector", lambda e: e.reciprocal(st16b[:, :], st16a[:, :]))
    _v = sc.now("v")
    sc.wait("scalar", "v", _v)
    sc.emit("scalar", lambda e: e.activation(
        stats_rinv[:, :], st16b[:, :], AF.Sqrt), sem="s")
    sc.wait("vector", "v", _v)
    sc.emit("vector", lambda e: e.tensor_scalar(
        st16c[:, :], st16a[:, :], -1e-5, None, ALU.add), sem="v")
    _v = sc.now("v")
    sc.wait("scalar", "v", _v)
    sc.emit("scalar", lambda e: e.activation(
        st16d[:, :], st16c[:, :], AF.Ln))
    sc.emit("scalar", lambda e: e.activation(
        st16c[:, :], st16a[:, :], AF.Ln), sem="s")
    sc.wait_now("vector", "s")
    sc.emit("vector", lambda e: e.scalar_tensor_tensor(
        st16c[:, :], st16d[:, :], 1.0, st16c[:, :],
        ALU.mult, ALU.subtract), sem="v")
    sc.wait_now("scalar", "v")
    sc.emit("scalar", lambda e: e.activation(
        st16c[:, :], st16c[:, :], AF.Exp), sem="s")
    sc.wait_now("vector", "s")
    sc.emit("vector", lambda e: e.tensor_scalar(
        st16c[:, :], st16c[:, :], 1e-6, None, ALU.add), sem="v")
    sc.wait_now("scalar", "v")
    sc.emit("scalar", lambda e: e.activation(
        st16c[:, :], st16c[:, :], AF.Ln))
    sc.emit("scalar", lambda e: e.activation(
        st16c[:, :], st16c[:, :], AF.Exp, scale=-0.5), sem="s")
    sc.wait_now("vector", "s")
    sc.emit("vector", lambda e: e.scalar_tensor_tensor(
        stats_r3[:, :], stats_rinv[:, :], 1.0, st16c[:, :],
        ALU.mult, ALU.mult), sem="v")
    r3_done = sc.now("v")

    xr_tp_arr = [0] * NT
    vdone_xr = [0] * NT
    for j in range(NT):
        sc.wait("vector", "v", r3_done)
        sc.emit("vector", lambda e, j=j: e.tensor_scalar(
            xrw[j % 2][:, :], xc_full[:, j * D:(j + 1) * D],
            stats_r3[:, j:j + 1], None, ALU.mult))
        sc.emit("vector", lambda e, j=j: e.tensor_tensor(
            xrw[j % 2][:, :], xrw[j % 2][:, :], wn_bc[:, :], ALU.mult), sem="v")
        _v = sc.now("v")
        sc.wait("gpsimd", "v", _v)
        if j >= 2:
            sc.wait("gpsimd", "si", xr_tp_arr[j - 2])
        cast = sc.dma("gpsimd", lambda e, j=j: e.dma_start(
            out=xrh_t[j % 2][:, :], in_=xrw[j % 2][:, :]))
        sc.wait("vector", "sg", cast)
        if j >= 2:
            sc.wait("vector", "si", xr_tp_arr[j - 2])
        sc.emit("vector", lambda e, j=j: e.tensor_tensor(
            xrl_t[j % 2][:, :], xrw[j % 2][:, :], xrh_t[j % 2][:, :],
            ALU.subtract), sem="v")
        vdone_xr[j] = sc.now("v")
        sc.wait("sync", "v", vdone_xr[j])
        sc.dma("sync", lambda e, j=j: e.dma_start_transpose(
            xrh_fm[:, :].rearrange("p (c t) -> p c t", c=KD)[:, :, j * 128:(j + 1) * 128],
            xrh_t[j % 2][:, :].rearrange("t (c d) -> t c d", c=KD)))
        xr_tp_arr[j] = sc.dma("sync", lambda e, j=j: e.dma_start_transpose(
            xrl_fm[:, :].rearrange("p (c t) -> p c t", c=KD)[:, :, j * 128:(j + 1) * 128],
            xrl_t[j % 2][:, :].rearrange("t (c d) -> t c d", c=KD)))
    xr_tp_done = xr_tp_arr[NT - 1]

    sc.wait_now("vector", "v")
    sc.emit("vector", lambda e: e.scalar_tensor_tensor(
        cmax16[:, :], cmax16[:, :], 1.0, stats_rinv[:, :],
        ALU.mult, ALU.mult))
    sc.slow("vector", lambda e: e.reduce_max(
        pmax1[:, :], cmax16[:, :], axis=AX.X))
    _v = sc.now("v")
    sc.wait("gpsimd", "v", _v)
    b1 = sc.dma("gpsimd", lambda e: e.dma_start(out=red_d, in_=pmax1[:, :]))
    sc.wait("gpsimd", "sg", b1)
    b2 = sc.dma("gpsimd", lambda e: e.dma_start(
        out=pm_row[0:1, :], in_=red_d.rearrange("p one -> one p")))
    sc.wait("vector", "sg", b2)
    sc.slow("vector", lambda e: e.reduce_max(
        gm1[0:1, 0:1], pm_row[0:1, :], axis=AX.X))
    _v = sc.now("v")
    sc.wait("gpsimd", "v", _v)
    b3 = sc.dma("gpsimd", lambda e: e.dma_start(out=cc1i.ap(), in_=gm1[0:1, 0:1]))
    sc.wait("gpsimd", "sg", b3)
    sc.emit("gpsimd", lambda e: e.collective_compute(
        "AllReduce", ALU.max, replica_groups=[list(range(NCORES))],
        ins=[cc1i.ap().opt()], outs=[cc1o.ap().opt()]), sem="cc")
    sc.wait("gpsimd", "cc", sc.now("cc"))
    b4 = sc.dma("gpsimd", lambda e: e.dma_start(
        out=gmx_col[:, :], in_=cc1o.ap().broadcast_to([128, 1])))

    sc.wait("scalar", "si", sc.now("si"))
    sc.emit("scalar", lambda e: e.activation(
        sg_col[:, :], clipx_col[:, :], AF.Sigmoid), sem="s")
    sc.wait("vector", "s", sc.now("s"))
    sc.wait("vector", "sg", b4)
    sc.emit("vector", lambda e: e.scalar_tensor_tensor(
        sx_col[:, :], gmx_col[:, :], 1.0 / QMAX, sg_col[:, :],
        ALU.mult, ALU.mult), sem="v")
    sc.slow("vector", lambda e: e.reciprocal(
        inv_sx_col[:, :], sx_col[:, :]))

    sc.emit("vector", lambda e: e.tensor_scalar(
        qcol[:, :], stats_rinv[:, :], inv_sx_col[:, :], None, ALU.mult), sem="v")
    sc.wait_now("vector", "v")
    xq_tp_arr = [0] * NT
    for j in range(NT):
        sc.emit("vector", lambda e, j=j: e.tensor_scalar(
            q0t[:, :], xc_full[:, j * D:(j + 1) * D], qcol[:, j:j + 1],
            MAGIC, ALU.mult, ALU.add))
        sc.emit("vector", lambda e: e.tensor_scalar(
            q0t[:, :], q0t[:, :], -MAGIC, -QMAX - 1.0, ALU.add, ALU.max))
        if j >= 2:
            sc.wait("vector", "si", xq_tp_arr[j - 2])
        sc.emit("vector", lambda e, j=j: e.tensor_scalar(
            qbt[j % 2][:, :], q0t[:, :], QMAX, None, ALU.min), sem="v")
        _v = sc.now("v")
        sc.wait("sync", "v", _v)
        xq_tp_arr[j] = sc.dma("sync", lambda e, j=j: e.dma_start_transpose(
            xq_fm[:, :].rearrange("p (c t) -> p c t", c=KD)[:, :, j * 128:(j + 1) * 128],
            qbt[j % 2][:, :].rearrange("t (c d) -> t c d", c=KD)))

    sc.wait("gpsimd", "si", xq_tp_arr[NT - 1])
    sc.dma("gpsimd", lambda e: e.dma_start(out=xq_dram, in_=xq_fm[:, :]))

    sc.barrier()
    p0.free()

    pooly = Pool(nc, side="right")
    y_hi = pooly.sb("y_hi", [128, KD * T], BF16)
    y_lo = pooly.sb("y_lo", [128, KD * T], BF16)
    p1 = Pool(nc, side="right")
    wblk = [p1.sb(f"wblk{i}", [128, KD * 128]) for i in range(2)]
    wh = [[p1.sb(f"wh{g}{i}", [128, KD * 128], BF16) for i in range(2)] for g in range(3)]
    wl = [[p1.sb(f"wl{g}{i}", [128, KD * 128], BF16) for i in range(2)] for g in range(3)]
    a_c = p1.sb("a_c", [128, T])
    u_c = p1.sb("u_c", [128, T])
    y_c = p1.sb("y_c", [128, T])
    s1_t = [p1.sb("s1_t0", [128, 512])] * 2
    r_t = [p1.sb("r_t0", [128, 512])] * 2
    v_t = [p1.sb("v_t0", [128, 512])] * 2
    t1_t = [p1.sb("t1_t0", [128, 512])] * 2
    pg = [[p1.ps(f"pg{g}{i}", [128, 512]) for i in range(2)] for g in range(3)]

    win_r = win_d

    wsplit_done = {}
    wsplit_lin = []
    evac_done = {}
    scan_done = [0] * KD
    p1_last_grp = {}

    def p1_split(c):
        for g in range(3):
            eb = g * KD + c
            b = c * 3 + g
            wb = b % 2
            if b >= 2:
                sc.wait("sync", "v", wsplit_lin[b - 2])
            ld = sc.dma("sync", lambda e, wb=wb, eb=eb: e.dma_start(
                out=wblk[wb][:, :].rearrange("p (k e) -> p k e", k=KD),
                in_=win_r[eb]))
            sc.wait("gpsimd", "si", ld)
            if c >= 2:
                sc.wait("gpsimd", "t", p1_last_grp[c - 2])
            cast = sc.dma("gpsimd", lambda e, g=g, c=c, wb=wb: e.dma_start(
                out=wh[g][c % 2][:, :], in_=wblk[wb][:, :]))
            sc.wait("vector", "sg", cast)
            if c >= 2:
                sc.wait("vector", "t", p1_last_grp[c - 2])
            sc.emit("vector", lambda e, g=g, c=c, wb=wb: e.tensor_tensor(
                wl[g][c % 2][:, :], wblk[wb][:, :], wh[g][c % 2][:, :],
                ALU.subtract), sem="v")
            wsplit_done[(c, g)] = sc.now("v")
            wsplit_lin.append(sc.now("v"))

    p1_split(0)
    for c in range(KD):
        for t in range(T4):
            for g in range(3):
                sc.wait("tensor", "v", wsplit_done[(c, g)])
                if c == 0 and t == 0 and g == 0:
                    sc.wait("tensor", "si", xr_tp_done)
                key = (c, t - 2) if t >= 2 else (c - 1, t + 2)
                if key in evac_done:
                    sc.wait("tensor", "s", evac_done[key][0])
                    sc.wait("tensor", "v", evac_done[key][1])
                for pi, (lhs, rhs) in enumerate(
                        [(wh, xrh_fm), (wh, xrl_fm), (wl, xrh_fm)]):
                    for k in range(KD):
                        sc.emit("tensor", lambda e, g=g, c=c, t=t, k=k, pi=pi,
                                lhs=lhs, rhs=rhs: e.matmul(
                            pg[g][t % 2][:, :],
                            lhs[g][c % 2][:, k * 128:(k + 1) * 128],
                            rhs[:, k * T + t * 512:k * T + (t + 1) * 512],
                            start=(pi == 0 and k == 0),
                            stop=(pi == 2 and k == KD - 1)),
                            sem=("t" if (pi == 2 and k == KD - 1) else None))
            grp = sc.now("t")
            sc.wait("scalar", "t", grp)
            if t >= 1:
                sc.wait("scalar", "v", evac_done[(c, t - 1)][1])
            if t == 0 and c >= 1:
                sc.wait("scalar", "v", scan_done[c - 1])
            sl = (t * 512, (t + 1) * 512)
            sc.emit("scalar", lambda e, c=c, t=t, sl=sl: e.activation(
                a_c[:, sl[0]:sl[1]], pg[0][t % 2][:, :], AF.Sigmoid))
            sc.emit("scalar", lambda e, c=c, t=t: e.activation(
                s1_t[t % 2][:, :], pg[0][t % 2][:, :], AF.Sigmoid, scale=-1.0))
            sc.emit("scalar", lambda e, c=c, t=t: e.activation(
                r_t[t % 2][:, :], pg[1][t % 2][:, :], AF.Sigmoid))
            sc.emit("scalar", lambda e, c=c, t=t: e.activation(
                v_t[t % 2][:, :], pg[2][t % 2][:, :], AF.Gelu), sem="s")
            s_ev = sc.now("s")
            sc.wait("vector", "s", s_ev)
            sc.emit("vector", lambda e, t=t: e.tensor_tensor(
                t1_t[t % 2][:, :], r_t[t % 2][:, :], v_t[t % 2][:, :], ALU.mult))
            sc.emit("vector", lambda e, t=t, sl=sl: e.tensor_tensor(
                u_c[:, sl[0]:sl[1]], t1_t[t % 2][:, :], s1_t[t % 2][:, :],
                ALU.mult), sem="v")
            evac_done[(c, t)] = (s_ev, sc.now("v"))
            if t == 0 and c + 1 < KD:
                p1_split(c + 1)
        p1_last_grp[c] = sc.now("t")
        sc.slow("vector", lambda e, c=c: e.tensor_tensor_scan(
            y_c[:, :], a_c[:, :], u_c[:, :], 0.0, ALU.mult, ALU.add))
        sc.emit("vector", lambda e, c=c: e.tensor_copy(
            y_hi[:, c * T:(c + 1) * T], y_c[:, :]))
        sc.emit("vector", lambda e, c=c: e.tensor_tensor(
            y_lo[:, c * T:(c + 1) * T], y_c[:, :], y_hi[:, c * T:(c + 1) * T],
            ALU.subtract), sem="v")
        scan_done[c] = sc.now("v")

    sc.barrier()
    p1.free()
    poolA.free()

    poolg = Pool(nc)
    qwgT = poolg.sb("qwgT", [128, KD * I_], BF16)
    p2a = Pool(nc, side="right")
    woh = p2a.sb("woh", [128, KD * D], BF16)
    wol = p2a.sb("wol", [128, KD * D], BF16)
    attn_sc = [p2a.sb("attn_sc0", [128, D])] * 2
    pw = [p2a.ps(f"pw{i}", [128, D]) for i in range(2)]
    wg_st = [p2a.sb(f"wg_st{i}", [128, D]) for i in range(2)]
    wq0 = p2a.sb("wq0", [128, D])
    wqb = [p2a.sb(f"wqb{i}", [128, D], BF16) for i in range(2)]
    rmaxc = p2a.sb("rmaxc", [128, 2])
    p2w = Pool(nc, side="right")
    wo_st = p2w.sb("wo_st", [128, KD * D])

    sc.wait("scalar", "si", sc.now("si"))
    sc.emit("scalar", lambda e: e.activation(
        lam_col[:, :], lam_col[:, :], AF.Sigmoid), sem="s")
    wo_ld = sc.dma("sync", lambda e: e.dma_start(
        out=wo_st[:, :].rearrange("p (k d) -> p k d", k=KD),
        in_=wout_d.rearrange("(k p) d -> p k d", p=128)))
    sc.wait("vector", "si", wo_ld)
    sc.wait("vector", "s", sc.now("s"))
    for k in range(KD):
        sc.emit("vector", lambda e, k=k: e.tensor_scalar(
            wo_st[:, k * D:(k + 1) * D], wo_st[:, k * D:(k + 1) * D],
            lam_col[:, k:k + 1], None, ALU.mult),
            sem=("v" if k == KD - 1 else None))
    _v = sc.now("v")
    sc.wait("gpsimd", "v", _v)
    wo_cast = sc.dma("gpsimd", lambda e: e.dma_start(out=woh[:, :], in_=wo_st[:, :]))
    sc.wait("vector", "sg", wo_cast)
    sc.emit("vector", lambda e: e.tensor_tensor(
        wol[:, :], wo_st[:, :], woh[:, :], ALU.subtract), sem="v")
    wo_split = sc.now("v")
    p2w.free()

    poolwd = Pool(nc)
    wd_st2 = poolwd.sb("wd_st2", [128, I_])
    dqb2 = poolwd.sb("dqb2", [128, I_], BF16)
    wd_stage = poolwd.sb("wd_stage", [128, KI * 128], BF16)

    wout_evac = {}
    qwg_tp = [0] * KI

    def p2_wout_step(tc):
        sc.wait("tensor", "v", wo_split)
        if tc >= 2:
            sc.wait("tensor", "v", wout_evac[tc - 2])
        n = 0
        for nh in range(2):
            for pi, (lhs, rhs) in enumerate([(y_hi, woh), (y_hi, wol), (y_lo, woh)]):
                for k in range(KD):
                    n += 1
                    sc.emit("tensor", lambda e, tc=tc, nh=nh, k=k, pi=pi,
                            lhs=lhs, rhs=rhs: e.matmul(
                        pw[tc % 2][:, nh * 512:(nh + 1) * 512],
                        lhs[:, k * T + tc * 128:k * T + (tc + 1) * 128],
                        rhs[:, k * D + nh * 512:k * D + (nh + 1) * 512],
                        start=(pi == 0 and k == 0),
                        stop=(pi == 2 and k == KD - 1)),
                        sem=("t" if (pi == 2 and k == KD - 1) else None))
        grp = sc.now("t")
        sc.wait("vector", "t", grp)
        if tc >= 1:
            sc.wait("vector", "sg", asp_done[tc - 1])
        sc.emit("vector", lambda e, tc=tc: e.tensor_copy(
            attn_sc[tc % 2][:, :], pw[tc % 2][:, :]), sem="v")
        wout_evac[tc] = sc.now("v")
        sc.wait("gpsimd", "v", wout_evac[tc])
        asp_done[tc] = sc.dma("gpsimd", lambda e, tc=tc: e.dma_start(
            out=asp_d[tc], in_=attn_sc[tc % 2][:, :]))

    wgq_v = {}

    def p2_wgq_step(ic):
        if ic >= 2:
            sc.wait("sync", "v", wgq_v[ic - 2])
        ld = sc.dma("sync", lambda e, ic=ic: e.dma_start(
            out=wg_st[ic % 2][:, :], in_=wg_d[ic * 128:(ic + 1) * 128, :]))
        sc.wait("vector", "si", ld)
        sc.slow("vector", lambda e, ic=ic: e.reduce_max(
            rmaxc[:, ic % 2:ic % 2 + 1], wg_st[ic % 2][:, :], axis=AX.X,
            apply_absolute_value=True))
        sc.emit("vector", lambda e, ic=ic: e.tensor_scalar(
            comb_g[:, ic:ic + 1], rmaxc[:, ic % 2:ic % 2 + 1],
            sx_col[:, :], 1.0 / QMAX, ALU.mult, ALU.mult))
        sc.slow("vector", lambda e, ic=ic: e.reciprocal(
            rmaxc[:, ic % 2:ic % 2 + 1], rmaxc[:, ic % 2:ic % 2 + 1]))
        sc.emit("vector", lambda e, ic=ic: e.tensor_scalar(
            wq0[:, :], wg_st[ic % 2][:, :], rmaxc[:, ic % 2:ic % 2 + 1],
            QMAX, ALU.mult, ALU.mult))
        if ic >= 2:
            sc.wait("vector", "si", qwg_tp[ic - 2])
        sc.emit("vector", lambda e, ic=ic: e.tensor_scalar(
            wqb[ic % 2][:, :], wq0[:, :], MAGIC, -MAGIC, ALU.add, ALU.add),
            sem="v")
        wgq_v[ic] = sc.now("v")
        sc.wait("sync", "v", wgq_v[ic])
        qwg_tp[ic] = sc.dma("sync", lambda e, ic=ic: e.dma_start_transpose(
            qwgT[:, :].rearrange("p (k i) -> p k i", k=KD)[:, :, ic * 128:(ic + 1) * 128],
            wqb[ic % 2][:, :].rearrange("i (k h) -> i k h", k=KD)))

    wdq_done = [0] * KD
    wdtp_ss = [0] * KD
    wdsp_ss = [0] * KD

    def p2_wdred_step(oc):
        if oc == 0:
            sc.wait("scalar", "v", wo_split)
        if oc >= 1:
            sc.wait("scalar", "v", wdq_done[oc - 1])
        ldw = sc.dma("scalar", lambda e, oc=oc: e.dma_start(
            out=wd_st2[:, :], in_=wd_d[oc * 128:(oc + 1) * 128, :]))
        sc.wait("vector", "ss", ldw)
        sc.slow("vector", lambda e, oc=oc: e.reduce_max(
            drmax8_p[:, oc:oc + 1], wd_st2[:, :], axis=AX.X,
            apply_absolute_value=True))
        sc.emit("vector", lambda e, oc=oc: e.tensor_scalar(
            comb_d[:, oc:oc + 1], drmax8_p[:, oc:oc + 1], 1.0, None, ALU.mult))
        sc.emit("vector", lambda e, oc=oc: e.tensor_scalar(
            rinv127_p[:, oc:oc + 1], drmax8_p[:, oc:oc + 1], 1.0 / QMAX, None,
            ALU.mult))
        sc.slow("vector", lambda e, oc=oc: e.reciprocal(
            rinv127_p[:, oc:oc + 1], rinv127_p[:, oc:oc + 1]))
        sc.emit("vector", lambda e, oc=oc: e.tensor_scalar(
            wd_st2[:, :], wd_st2[:, :], rinv127_p[:, oc:oc + 1], MAGIC,
            ALU.mult, ALU.add))
        if oc >= 1:
            sc.wait("vector", "ss", wdtp_ss[oc - 1])
        sc.emit("vector", lambda e, oc=oc: e.tensor_scalar(
            dqb2[:, :], wd_st2[:, :], -MAGIC, None, ALU.add), sem="v")
        wdq_done[oc] = sc.now("v")
        sc.wait("scalar", "v", wdq_done[oc])
        if oc >= 1:
            sc.wait("scalar", "ss", wdsp_ss[oc - 1])
        wdtp_ss[oc] = sc.dma("scalar", lambda e, oc=oc: e.dma_start_transpose(
            wd_stage[:, :].rearrange("p (k o) -> p k o", k=KI),
            dqb2[:, :].rearrange("o (k h) -> o k h", k=KI)))
        sc.wait("scalar", "ss", wdtp_ss[oc])
        wdsp_ss[oc] = sc.dma("scalar", lambda e, oc=oc: e.dma_start(
            out=qwdT_d.rearrange("p (k o) -> p k o", k=KI)[:, :, oc * 128:(oc + 1) * 128],
            in_=wd_stage[:, :].rearrange("p (k o) -> p k o", k=KI)))

    asp_done = {}
    steps = []
    for i in range(max(NT, KI)):
        if i < NT:
            steps.append(("w", i))
        if i < KI:
            steps.append(("q", i))
        if i % 2 == 1 and i // 2 < KD:
            steps.append(("d", i // 2))
    for kind, i in steps:
        if kind == "w":
            p2_wout_step(i)
        elif kind == "q":
            p2_wgq_step(i)
        else:
            p2_wdred_step(i)

    sc.barrier(sems=("si", "sg", "v", "s", "t"))
    p2a.free()
    pooly.free()

    poolxq2 = Pool(nc)
    xq_fm2 = poolxq2.sb("xq_fm2", [128, KD * T], BF16)
    xq_ld = sc.dma("gpsimd", lambda e: e.dma_start(out=xq_fm2[:, :], in_=xq_dram))
    p3 = Pool(nc, side="right")
    qwuT = p3.sb("qwuT", [128, KD * I_], BF16)
    wu_st = [p3.sb(f"wu_st{i}", [128, D]) for i in range(2)]
    uq0 = p3.sb("uq0", [128, D])
    uqb = [p3.sb(f"uqb{i}", [128, D], BF16) for i in range(2)]
    urmax = p3.sb("urmax", [128, 2])
    sg_t = [p3.sb(f"sg_t{i}", [128, 512]) for i in range(2)]
    uu_t = [p3.sb(f"uu_t{i}", [128, 512]) for i in range(2)]
    h_ch = [p3.sb(f"h_ch{i}", [128, T]) for i in range(2)]
    hrun = p3.sb("hrun", [128, 1])
    habs = p3.sb("habs", [128, 1])
    pgu = [[p3.ps(f"pgu{a}{i}", [128, 512]) for i in range(2)] for a in range(2)]

    sc.emit("vector", lambda e: e.memset(hrun[:, :], 0.0))

    qwu_tp = [0] * KI
    uq_v = {}

    def p3_wuq_step(ic):
        if ic >= 2:
            sc.wait("sync", "v", uq_v[ic - 2])
        ld = sc.dma("sync", lambda e, ic=ic: e.dma_start(
            out=wu_st[ic % 2][:, :], in_=wu_d[ic * 128:(ic + 1) * 128, :]))
        sc.wait("vector", "si", ld)
        sc.slow("vector", lambda e, ic=ic: e.reduce_max(
            urmax[:, ic % 2:ic % 2 + 1], wu_st[ic % 2][:, :], axis=AX.X,
            apply_absolute_value=True))
        sc.emit("vector", lambda e, ic=ic: e.tensor_scalar(
            comb_u[:, ic:ic + 1], urmax[:, ic % 2:ic % 2 + 1],
            sx_col[:, :], 1.0 / QMAX, ALU.mult, ALU.mult))
        sc.slow("vector", lambda e, ic=ic: e.reciprocal(
            urmax[:, ic % 2:ic % 2 + 1], urmax[:, ic % 2:ic % 2 + 1]))
        sc.emit("vector", lambda e, ic=ic: e.tensor_scalar(
            uq0[:, :], wu_st[ic % 2][:, :], urmax[:, ic % 2:ic % 2 + 1],
            QMAX, ALU.mult, ALU.mult))
        if ic >= 2:
            sc.wait("vector", "si", qwu_tp[ic - 2])
        sc.emit("vector", lambda e, ic=ic: e.tensor_scalar(
            uqb[ic % 2][:, :], uq0[:, :], MAGIC, -MAGIC, ALU.add, ALU.add),
            sem="v")
        uq_v[ic] = sc.now("v")
        sc.wait("sync", "v", uq_v[ic])
        qwu_tp[ic] = sc.dma("sync", lambda e, ic=ic: e.dma_start_transpose(
            qwuT[:, :].rearrange("p (k i) -> p k i", k=KD)[:, :, ic * 128:(ic + 1) * 128],
            uqb[ic % 2][:, :].rearrange("i (k h) -> i k h", k=KD)))

    p3_ev = {}
    p3_tt = {}
    hsp_done = {}
    for ic in range(KI):
        if ic == 0:
            p3_wuq_step(0)
        for t in range(T4):
            sc.wait("tensor", "si", qwu_tp[ic])
            if ic == 0 and t == 0:
                sc.wait("tensor", "sg", xq_ld)
            key = (ic, t - 2) if t >= 2 else (ic - 1, t + 2)
            if key in p3_ev:
                sc.wait("tensor", "s", p3_ev[key])
            for ai, wT in enumerate((qwgT, qwuT)):
                for k in range(KD):
                    sc.emit("tensor", lambda e, ic=ic, t=t, k=k, ai=ai, wT=wT:
                            e.matmul(
                        pgu[ai][t % 2][:, :],
                        wT[:, k * I_ + ic * 128:k * I_ + (ic + 1) * 128],
                        xq_fm2[:, k * T + t * 512:k * T + (t + 1) * 512],
                        start=(k == 0), stop=(k == KD - 1)),
                        sem=("t" if (ai == 1 and k == KD - 1) else None))
            grp = sc.now("t")
            sc.wait("scalar", "t", grp)
            if key in p3_tt:
                sc.wait("scalar", "v", p3_tt[key])
            sc.emit("scalar", lambda e, ic=ic, t=t: e.activation(
                sg_t[t % 2][:, :], pgu[0][t % 2][:, :], AF.Silu,
                scale=comb_g[:, ic:ic + 1]))
            sc.emit("scalar", lambda e, ic=ic, t=t: e.activation(
                uu_t[t % 2][:, :], pgu[1][t % 2][:, :], AF.Identity,
                scale=comb_u[:, ic:ic + 1]), sem="s")
            p3_ev[(ic, t)] = sc.now("s")
            sc.wait("vector", "s", p3_ev[(ic, t)])
            if t == 0 and ic >= 2:
                sc.wait("vector", "sg", hsp_done[ic - 2])
            sc.emit("vector", lambda e, ic=ic, t=t: e.tensor_tensor(
                h_ch[ic % 2][:, t * 512:(t + 1) * 512], sg_t[t % 2][:, :],
                uu_t[t % 2][:, :], ALU.mult), sem="v")
            p3_tt[(ic, t)] = sc.now("v")
            if t == 0 and ic + 1 < KI:
                p3_wuq_step(ic + 1)
        sc.slow("vector", lambda e, ic=ic: e.reduce_max(
            habs[:, :], h_ch[ic % 2][:, :], axis=AX.X,
            apply_absolute_value=True))
        sc.emit("vector", lambda e: e.scalar_tensor_tensor(
            hrun[:, :], habs[:, :], 1.0, hrun[:, :], ALU.mult, ALU.max),
            sem="v")
        _v = sc.now("v")
        sc.wait("gpsimd", "v", _v)
        hsp_done[ic] = sc.dma("gpsimd", lambda e, ic=ic: e.dma_start(
            out=hsp_d.rearrange("g p x -> p g x")[:, :, ic * 256:(ic + 1) * 256],
            in_=h_ch[ic % 2][:, :].rearrange("p (g t) -> p g t", g=T // 256)))

    _v = sc.now("v")
    sc.wait("gpsimd", "v", _v)
    c1 = sc.dma("gpsimd", lambda e: e.dma_start(out=red_d, in_=hrun[:, :]))
    sc.wait("gpsimd", "sg", c1)
    c2 = sc.dma("gpsimd", lambda e: e.dma_start(
        out=pm_row[0:1, :], in_=red_d.rearrange("p one -> one p")))
    sc.wait("vector", "sg", c2)
    sc.slow("vector", lambda e: e.reduce_max(
        gm1[0:1, 0:1], pm_row[0:1, :], axis=AX.X))
    _v = sc.now("v")
    sc.wait("gpsimd", "v", _v)
    c3 = sc.dma("gpsimd", lambda e: e.dma_start(out=cc2i.ap(), in_=gm1[0:1, 0:1]))
    sc.wait("gpsimd", "sg", c3)
    sc.emit("gpsimd", lambda e: e.collective_compute(
        "AllReduce", ALU.max, replica_groups=[list(range(NCORES))],
        ins=[cc2i.ap().opt()], outs=[cc2o.ap().opt()]), sem="cc")
    cc2_done = sc.now("cc")

    sc.barrier()
    p3.free()
    poolxq2.free()
    poolwd.free()
    poolg.free()

    p4 = Pool(nc, side="right")
    qwdT = p4.sb("qwdT", [128, KI * D], BF16)
    h_g = [p4.sb(f"h_g{i}", [128, KI * 256]) for i in range(2)]
    qh_g = [p4.sb(f"qh_g{i}", [128, KI * 256], BF16) for i in range(2)]
    x_t4 = [p4.sb(f"x_t4{i}", [128, D]) for i in range(2)]
    xc_t4 = p4.sb("xc_t4", [128, D])
    xn_t4 = [p4.sb(f"xn_t4{i}", [128, D]) for i in range(2)]
    at_t4 = [p4.sb(f"at_t4{i}", [128, D]) for i in range(2)]
    o_t4 = [p4.sb(f"o_t4{i}", [128, D]) for i in range(2)]
    pd = [p4.ps(f"pd{i}", [128, D]) for i in range(2)]

    NG = T // 256

    qwd_a = sc.dma("sync", lambda e: e.dma_start(
        out=qwdT[:, 0:11 * D], in_=qwdT_d[:, 0:11 * D]))
    qwd_b = sc.dma("gpsimd", lambda e: e.dma_start(
        out=qwdT[:, 11 * D:], in_=qwdT_d[:, 11 * D:]))

    sc.wait("gpsimd", "cc", cc2_done)
    c4 = sc.dma("gpsimd", lambda e: e.dma_start(
        out=gmx_col[:, :], in_=cc2o.ap().broadcast_to([128, 1])))
    sc.emit("scalar", lambda e: e.activation(
        sg_col[:, :], clipd_col[:, :], AF.Sigmoid), sem="s")
    sc.wait("vector", "s", sc.now("s"))
    sc.wait("vector", "sg", c4)
    sc.emit("vector", lambda e: e.scalar_tensor_tensor(
        sh_col[:, :], gmx_col[:, :], 1.0 / QMAX, sg_col[:, :],
        ALU.mult, ALU.mult), sem="v")
    sc.slow("vector", lambda e: e.reciprocal(
        inv_sh_col[:, :], sh_col[:, :]))

    hq_g = [0] * NG
    grp_g = [0] * NG
    hgl_arr = [0] * NG

    def h_load(g):
        eng = "scalar" if g % 2 == 0 else "gpsimd"
        if g >= 2:
            sc.wait(eng, "v", hq_g[g - 2])
        hgl_arr[g] = (("ss" if g % 2 == 0 else "sg"),
                      sc.dma(eng, lambda e, g=g: e.dma_start(
                          out=h_g[g % 2][:, :], in_=hsp_d[g])))

    def h_quant(g):
        sc.wait("vector", hgl_arr[g][0], hgl_arr[g][1])
        if g >= 2:
            sc.wait("vector", "t", grp_g[g - 2])
        sc.emit("vector", lambda e, g=g: e.tensor_scalar(
            h_g[g % 2][:, :], h_g[g % 2][:, :], inv_sh_col[:, :], MAGIC,
            ALU.mult, ALU.add))
        sc.emit("vector", lambda e, g=g: e.tensor_scalar(
            h_g[g % 2][:, :], h_g[g % 2][:, :], -MAGIC, -QMAX - 1.0,
            ALU.add, ALU.max))
        sc.emit("vector", lambda e, g=g: e.tensor_scalar(
            qh_g[g % 2][:, :], h_g[g % 2][:, :], QMAX, None, ALU.min), sem="v")
        hq_g[g] = sc.now("v")

    h_load(0)
    h_load(1)

    sc.emit("vector", lambda e: e.tensor_scalar(
        comb_d[:, :], comb_d[:, :], sh_col[:, :], 1.0 / QMAX,
        ALU.mult, ALU.mult), sem="v")


    _v = sc.now("v")
    sc.wait("gpsimd", "v", _v)
    d1b = sc.dma("gpsimd", lambda e: e.dma_start(
        out=os_d.rearrange("c p -> p c"), in_=comb_d[:, :],
        allow_slow_non_contiguous=True))
    sc.wait("gpsimd", "sg", d1b)
    d2b = sc.dma("gpsimd", lambda e: e.dma_start(
        out=oscale_bc[:, :],
        in_=os_d.rearrange("c p -> (c p)").rearrange("(a d) -> a d", a=1)
        .broadcast_to([128, D])))
    sc.wait("vector", "sg", d2b)

    h_quant(0)
    p4_ev = {}
    out_done = {}
    for g in range(NG):
        if g + 1 < NG:
            h_load(g + 1)
        for s in range(2):
            tc = g * 2 + s
            if tc >= 2:
                sc.wait("sync", "v", p4_ev[tc - 2][1])
            xld = sc.dma("sync", lambda e, tc=tc: e.dma_start(
                out=x_t4[tc % 2][:, :], in_=x_d[tc * 128:(tc + 1) * 128, :]))
            if tc >= 2:
                sc.wait("gpsimd", "v", p4_ev[tc - 2][1])
            ald = sc.dma("gpsimd", lambda e, tc=tc: e.dma_start(
                out=at_t4[tc % 2][:, :], in_=asp_d[tc]))
            sc.wait("tensor", "v", hq_g[g])
            if tc == 0:
                sc.wait("tensor", "si", qwd_a)
                sc.wait("tensor", "sg", qwd_b)
            if tc >= 2:
                sc.wait("tensor", "v", p4_ev[tc - 2][1])
            for nh in range(2):
                for k in range(KI):
                    sc.emit("tensor", lambda e, s=s, g=g, tc=tc, nh=nh, k=k: e.matmul(
                        pd[tc % 2][:, nh * 512:(nh + 1) * 512],
                        qh_g[g % 2][:, k * 256 + s * 128:k * 256 + (s + 1) * 128],
                        qwdT[:, k * D + nh * 512:k * D + (nh + 1) * 512],
                        start=(k == 0), stop=(k == KI - 1)),
                        sem=("t" if (nh == 1 and k == KI - 1) else None))
            grp = sc.now("t")
            grp_g[g] = grp
            if s == 0 and g + 1 < NG:
                h_quant(g + 1)
            sc.wait("scalar", "si", xld)
            sc.emit("scalar", lambda e, tc=tc: e.activation(
                xc_t4[:, :], x_t4[tc % 2][:, :], AF.Identity,
                bias=stats_negm[:, tc:tc + 1]))
            sc.emit("scalar", lambda e, tc=tc: e.activation(
                xn_t4[tc % 2][:, :], xc_t4[:, :], AF.Identity,
                scale=stats_rinv[:, tc:tc + 1]), sem="s")
            sres = sc.now("s")
            sc.wait("vector", "t", grp)
            sc.wait("vector", "s", sres)
            sc.wait("vector", "sg", ald)
            if tc >= 2:
                sc.wait("vector", "si", out_done[tc - 2])
            sc.emit("vector", lambda e, tc=tc: e.tensor_tensor(
                xn_t4[tc % 2][:, :], xn_t4[tc % 2][:, :], x_t4[tc % 2][:, :], ALU.add))
            sc.emit("vector", lambda e, tc=tc: e.tensor_tensor(
                o_t4[tc % 2][:, :], pd[tc % 2][:, :], oscale_bc[:, :], ALU.mult))
            sc.emit("vector", lambda e, tc=tc: e.tensor_tensor(
                o_t4[tc % 2][:, :], o_t4[tc % 2][:, :], at_t4[tc % 2][:, :], ALU.add))
            sc.emit("vector", lambda e, tc=tc: e.tensor_tensor(
                o_t4[tc % 2][:, :], o_t4[tc % 2][:, :], xn_t4[tc % 2][:, :], ALU.add),
                sem="v")
            p4_ev[tc] = (grp, sc.now("v"))
            sc.wait("sync", "v", p4_ev[tc][1])
            out_done[tc] = sc.dma("sync", lambda e, tc=tc: e.dma_start(
                out=out_d[tc * 128:(tc + 1) * 128, :], in_=o_t4[tc % 2][:, :]))

    sc.wait_now("sync", "si")
    sc.wait_now("gpsimd", "sg")

    semnames = ["si", "sg", "ss", "v", "s", "t", "cc"]
    for n in semnames:
        g = nc.semaphore(n)
        sc.semh[n] = g.__enter__()
    with nc.Block() as block:
        @block.sync
        def _(eng):
            for f in sc.ops["sync"]:
                f(eng)

        @block.vector
        def _(eng):
            for f in sc.ops["vector"]:
                f(eng)

        @block.scalar
        def _(eng):
            for f in sc.ops["scalar"]:
                f(eng)

        @block.tensor
        def _(eng):
            for f in sc.ops["tensor"]:
                f(eng)

        @block.gpsimd
        def _(eng):
            for f in sc.ops["gpsimd"]:
                f(eng)
    perm.free()
    p4.free()
    return nc


def kernel(**inputs):
    f32 = lambda k: np.ascontiguousarray(np.asarray(inputs[k], dtype=np.float32))
    x = f32("L0_attn_x")
    shared = {
        "L0_attn_w_norm": f32("L0_attn_w_norm"),
        "L0_attn_W_in": np.ascontiguousarray(
            f32("L0_attn_W_in").reshape(KD, 128, KE, 128).transpose(2, 1, 0, 3)),
        "L0_attn_lambda_log": f32("L0_attn_lambda_log"),
        "L0_attn_W_out": f32("L0_attn_W_out"),
        "L0_ffn_clip_x": f32("L0_ffn_clip_x"),
        "L0_ffn_clip_down": f32("L0_ffn_clip_down"),
        "L0_ffn_w_gate": f32("L0_ffn_w_gate"),
        "L0_ffn_w_up": f32("L0_ffn_w_up"),
        "L0_ffn_w_down": f32("L0_ffn_w_down"),
    }
    nc = build()
    in_maps = [dict(shared, L0_attn_x=np.ascontiguousarray(x[c]))
               for c in range(NCORES)]
    res = run_bass_kernel_spmd(nc, in_maps, core_ids=list(range(NCORES)))
    return np.stack(
        [np.asarray(r["out"], dtype=np.float32) for r in res.results], axis=0)


# version 34 (final)
# speedup vs baseline: 1.0576x; 1.0172x over previous
import numpy as np
import concourse.bass as bass
import concourse.mybir as mybir
from concourse.bass_utils import run_bass_kernel_spmd

F32 = mybir.dt.float32
BF16 = mybir.dt.bfloat16
I32 = mybir.dt.int32
AF = mybir.ActivationFunctionType
ALU = mybir.AluOpType
AX = mybir.AxisListType

MAGIC = 12582912.0
T = 2048
D = 1024
E = 3072
I_ = 2816
KD = D // 128
KE = E // 128
KI = I_ // 128
NT = T // 128
T4 = T // 512
NCORES = 8
QMAX = 127.0


class Sched:

    ENGS = ("sync", "vector", "scalar", "tensor", "gpsimd")

    def __init__(self):
        self.ops = {e: [] for e in self.ENGS}
        self.cnt = {}
        self.semh = {}

    def inc(self, sem, by):
        self.cnt[sem] = self.cnt.get(sem, 0) + by
        return self.cnt[sem]

    def now(self, sem):
        return self.cnt.get(sem, 0)

    def emit(self, eng, mk, sem=None, by=1):
        if sem is None:
            self.ops[eng].append(lambda e, mk=mk: mk(e))
        else:
            self.inc(sem, by)
            self.ops[eng].append(
                lambda e, mk=mk, s=sem, b=by: mk(e).then_inc(self.semh[s], b))

    def dma(self, eng, mk):
        sem = {"sync": "si", "scalar": "ss"}.get(eng, "sg")
        self.emit(eng, mk, sem=sem, by=16)
        return self.now(sem)

    def wait(self, eng, sem, val):
        if val <= 0:
            return
        self.ops[eng].append(
            lambda e, s=sem, v=val: e.wait_ge(self.semh[s], v))

    def wait_now(self, eng, sem):
        self.wait(eng, sem, self.now(sem))

    def slow(self, eng, mk, sem="v"):
        self.emit(eng, lambda e: e.engine_nop(), sem=sem)
        self.wait_now(eng, sem)
        self.emit(eng, mk, sem=sem)
        self.wait_now(eng, sem)

    def barrier(self, engines=None, sems=("si", "sg", "ss", "v", "s", "t")):
        engines = engines or self.ENGS
        vals = {s: self.now(s) for s in sems}
        for e in engines:
            for s, v in vals.items():
                self.wait(e, s, v)


class Pool:
    def __init__(self, nc, side="left"):
        self.nc = nc
        self.side = side
        self.guards = []

    def sb(self, name, shape, dt=F32):
        g = self.nc.sbuf_tensor(name, shape, dt, side=self.side)
        h = g.__enter__()
        self.guards.append(g)
        return h

    def ps(self, name, shape):
        g = self.nc.psum_tensor(name, shape, F32)
        h = g.__enter__()
        self.guards.append(g)
        return h

    def free(self):
        for g in reversed(self.guards):
            g.__exit__(None, None, None)
        self.guards = []


def build():
    nc = bass.Bass("TRN2", target_bir_lowering=False, detect_race_conditions=False)

    x_d = nc.dram_tensor("L0_attn_x", [T, D], F32, kind="ExternalInput").ap()
    wnorm_d = nc.dram_tensor("L0_attn_w_norm", [D], F32, kind="ExternalInput").ap()
    win_d = nc.dram_tensor("L0_attn_W_in", [KE, 128, KD, 128], F32, kind="ExternalInput").ap()
    lam_d = nc.dram_tensor("L0_attn_lambda_log", [D], F32, kind="ExternalInput").ap()
    wout_d = nc.dram_tensor("L0_attn_W_out", [D, D], F32, kind="ExternalInput").ap()
    clipx_d = nc.dram_tensor("L0_ffn_clip_x", [1], F32, kind="ExternalInput").ap()
    clipd_d = nc.dram_tensor("L0_ffn_clip_down", [1], F32, kind="ExternalInput").ap()
    wg_d = nc.dram_tensor("L0_ffn_w_gate", [I_, D], F32, kind="ExternalInput").ap()
    wu_d = nc.dram_tensor("L0_ffn_w_up", [I_, D], F32, kind="ExternalInput").ap()
    wd_d = nc.dram_tensor("L0_ffn_w_down", [D, I_], F32, kind="ExternalInput").ap()
    out_d = nc.dram_tensor("out", [T, D], F32, kind="ExternalOutput").ap()

    hsp_d = nc.dram_tensor("h_spill", [T // 256, 128, KI * 256], F32).ap()
    asp_d = nc.dram_tensor("attn_spill", [NT, 128, D], F32).ap()
    qwdT_d = nc.dram_tensor("qwdT_spill", [128, KI * D], BF16).ap()
    red_d = nc.dram_tensor("red_bounce", [128, 1], F32).ap()
    os_d = nc.dram_tensor("oscale_bounce", [KD, 128], F32).ap()
    cc1i = nc.dram_tensor("cc1_in", [1, 1], F32)
    cc1o = nc.dram_tensor("cc1_out", [1, 1], F32)
    cc2i = nc.dram_tensor("cc2_in", [1, 1], F32)
    cc2o = nc.dram_tensor("cc2_out", [1, 1], F32)

    sc = Sched()

    perm = Pool(nc)
    stats_negm = perm.sb("stats_negm", [128, NT])
    stats_rinv = perm.sb("stats_rinv", [128, NT])
    stats_r3 = perm.sb("stats_r3", [128, NT])
    cmax16 = perm.sb("cmax16", [128, NT])
    wn_bc = perm.sb("wn_bc", [128, D])
    oscale_bc = perm.sb("oscale_bc", [128, D])
    lam_col = perm.sb("lam_col", [128, KD])
    comb_g = perm.sb("comb_g", [128, KI])
    comb_u = perm.sb("comb_u", [128, KI])
    comb_d = perm.sb("comb_d", [128, KD])
    drmax8_p = perm.sb("drmax8_p", [128, KD])
    rinv127_p = perm.sb("rinv127_p", [128, KD])
    inv_sx_col = perm.sb("inv_sx_col", [128, 1])
    sx_col = perm.sb("sx_col", [128, 1])
    inv_sh_col = perm.sb("inv_sh_col", [128, 1])
    sh_col = perm.sb("sh_col", [128, 1])
    gm1 = perm.sb("gm1", [1, 1])
    gmx_col = perm.sb("gmx_col", [128, 1])
    sg_col = perm.sb("sg_col", [128, 1])
    clipx_col = perm.sb("clipx_col", [128, 1])
    clipd_col = perm.sb("clipd_col", [128, 1])
    pm_row = perm.sb("pm_row", [1, 128])

    xq_dram = nc.dram_tensor("xq_spill", [128, KD * T], BF16).ap()

    poolA = Pool(nc)
    xrh_fm = poolA.sb("xrh_fm", [128, KD * T], BF16)
    xrl_fm = poolA.sb("xrl_fm", [128, KD * T], BF16)

    p0 = Pool(nc, side="right")
    xq_fm = p0.sb("xq_fm", [128, KD * T], BF16)
    xc_full = p0.sb("xc_full", [128, NT * D])
    xst = [p0.sb(f"xst{i}", [128, D]) for i in range(2)]
    ssq = p0.sb("ssq", [128, NT])
    sumc = p0.sb("sumc", [128, 2])
    st16a = p0.sb("st16a", [128, NT])
    st16b = p0.sb("st16b", [128, NT])
    st16c = p0.sb("st16c", [128, NT])
    st16d = p0.sb("st16d", [128, NT])
    xrw = [p0.sb(f"xrw{i}", [128, D]) for i in range(2)]
    xrh_t = [p0.sb(f"xrh_t{i}", [128, D], BF16) for i in range(2)]
    xrl_t = [p0.sb(f"xrl_t{i}", [128, D], BF16) for i in range(2)]
    sqscr = xrw[0]
    q0t = xst[0]
    qbt = [p0.sb(f"qbt{i}", [128, D], BF16) for i in range(2)]
    qcol = p0.sb("qcol", [128, NT])
    pmax1 = p0.sb("pmax1", [128, 1])

    sc.dma("sync", lambda e: e.dma_start(
        out=wn_bc[:, :], in_=wnorm_d.rearrange("(a d) -> a d", a=1).broadcast_to([128, D])))
    sc.dma("sync", lambda e: e.dma_start(
        out=clipx_col[:, :],
        in_=clipx_d.rearrange("(a b) -> a b", a=1).broadcast_to([128, 1])))
    sc.dma("sync", lambda e: e.dma_start(
        out=clipd_col[:, :],
        in_=clipd_d.rearrange("(a b) -> a b", a=1).broadcast_to([128, 1])))
    sc.dma("sync", lambda e: e.dma_start(
        out=lam_col[:, :], in_=lam_d.rearrange("(c p) -> p c", p=128),
        allow_slow_non_contiguous=True))

    vdone_p0 = [0] * NT
    sdone_p0 = [0] * NT
    for j in range(NT):
        if j >= 2:
            sc.wait("sync", "s", sdone_p0[j - 2])
        ld = sc.dma("sync", lambda e, j=j: e.dma_start(
            out=xst[j % 2][:, :], in_=x_d[j * 128:(j + 1) * 128, :]))
        sc.wait("vector", "si", ld)
        sc.slow("vector", lambda e, j=j: e.reduce_sum(
            sumc[:, j % 2:j % 2 + 1], xst[j % 2][:, :], axis=AX.X))
        sc.emit("vector", lambda e, j=j: e.tensor_single_scalar(
            stats_negm[:, j:j + 1], sumc[:, j % 2:j % 2 + 1], -1.0 / D, ALU.mult),
            sem="v")
        vdone_p0[j] = sc.now("v")
        sc.wait("scalar", "v", vdone_p0[j])
        sc.emit("scalar", lambda e, j=j: e.activation(
            xc_full[:, j * D:(j + 1) * D], xst[j % 2][:, :], AF.Identity,
            bias=stats_negm[:, j:j + 1]))
        sc.emit("scalar", lambda e, j=j: e.activation(
            sqscr[:, :], xc_full[:, j * D:(j + 1) * D], AF.Square,
            accum_out=ssq[:, j:j + 1]), sem="s")
        sdone_p0[j] = sc.now("s")
        sc.wait("vector", "s", sdone_p0[j])
        sc.slow("vector", lambda e, j=j: e.reduce_max(
            cmax16[:, j:j + 1], xc_full[:, j * D:(j + 1) * D], axis=AX.X,
            apply_absolute_value=True))

    sc.wait_now("vector", "s")
    sc.emit("vector", lambda e: e.tensor_scalar(
        st16a[:, :], ssq[:, :], 1.0 / D, 1e-5, ALU.mult, ALU.add))
    sc.slow("vector", lambda e: e.reciprocal(st16b[:, :], st16a[:, :]))
    _v = sc.now("v")
    sc.wait("scalar", "v", _v)
    sc.emit("scalar", lambda e: e.activation(
        stats_rinv[:, :], st16b[:, :], AF.Sqrt), sem="s")
    sc.wait("vector", "v", _v)
    sc.emit("vector", lambda e: e.tensor_scalar(
        st16c[:, :], st16a[:, :], -1e-5, None, ALU.add), sem="v")
    _v = sc.now("v")
    sc.wait("scalar", "v", _v)
    sc.emit("scalar", lambda e: e.activation(
        st16d[:, :], st16c[:, :], AF.Ln))
    sc.emit("scalar", lambda e: e.activation(
        st16c[:, :], st16a[:, :], AF.Ln), sem="s")
    sc.wait_now("vector", "s")
    sc.emit("vector", lambda e: e.scalar_tensor_tensor(
        st16c[:, :], st16d[:, :], 1.0, st16c[:, :],
        ALU.mult, ALU.subtract), sem="v")
    sc.wait_now("scalar", "v")
    sc.emit("scalar", lambda e: e.activation(
        st16c[:, :], st16c[:, :], AF.Exp), sem="s")
    sc.wait_now("vector", "s")
    sc.emit("vector", lambda e: e.tensor_scalar(
        st16c[:, :], st16c[:, :], 1e-6, None, ALU.add), sem="v")
    sc.wait_now("scalar", "v")
    sc.emit("scalar", lambda e: e.activation(
        st16c[:, :], st16c[:, :], AF.Ln))
    sc.emit("scalar", lambda e: e.activation(
        st16c[:, :], st16c[:, :], AF.Exp, scale=-0.5), sem="s")
    sc.wait_now("vector", "s")
    sc.emit("vector", lambda e: e.scalar_tensor_tensor(
        stats_r3[:, :], stats_rinv[:, :], 1.0, st16c[:, :],
        ALU.mult, ALU.mult), sem="v")
    r3_done = sc.now("v")

    xr_tp_arr = [0] * NT
    vdone_xr = [0] * NT
    for j in range(NT):
        sc.wait("vector", "v", r3_done)
        sc.emit("vector", lambda e, j=j: e.tensor_scalar(
            xrw[j % 2][:, :], xc_full[:, j * D:(j + 1) * D],
            stats_r3[:, j:j + 1], None, ALU.mult))
        sc.emit("vector", lambda e, j=j: e.tensor_tensor(
            xrw[j % 2][:, :], xrw[j % 2][:, :], wn_bc[:, :], ALU.mult), sem="v")
        _v = sc.now("v")
        sc.wait("gpsimd", "v", _v)
        if j >= 2:
            sc.wait("gpsimd", "si", xr_tp_arr[j - 2])
        cast = sc.dma("gpsimd", lambda e, j=j: e.dma_start(
            out=xrh_t[j % 2][:, :], in_=xrw[j % 2][:, :]))
        sc.wait("vector", "sg", cast)
        if j >= 2:
            sc.wait("vector", "si", xr_tp_arr[j - 2])
        sc.emit("vector", lambda e, j=j: e.tensor_tensor(
            xrl_t[j % 2][:, :], xrw[j % 2][:, :], xrh_t[j % 2][:, :],
            ALU.subtract), sem="v")
        vdone_xr[j] = sc.now("v")
        sc.wait("sync", "v", vdone_xr[j])
        sc.dma("sync", lambda e, j=j: e.dma_start_transpose(
            xrh_fm[:, :].rearrange("p (c t) -> p c t", c=KD)[:, :, j * 128:(j + 1) * 128],
            xrh_t[j % 2][:, :].rearrange("t (c d) -> t c d", c=KD)))
        xr_tp_arr[j] = sc.dma("sync", lambda e, j=j: e.dma_start_transpose(
            xrl_fm[:, :].rearrange("p (c t) -> p c t", c=KD)[:, :, j * 128:(j + 1) * 128],
            xrl_t[j % 2][:, :].rearrange("t (c d) -> t c d", c=KD)))
    xr_tp_done = xr_tp_arr[NT - 1]

    sc.wait_now("vector", "v")
    sc.emit("vector", lambda e: e.scalar_tensor_tensor(
        cmax16[:, :], cmax16[:, :], 1.0, stats_rinv[:, :],
        ALU.mult, ALU.mult))
    sc.slow("vector", lambda e: e.reduce_max(
        pmax1[:, :], cmax16[:, :], axis=AX.X))
    _v = sc.now("v")
    sc.wait("gpsimd", "v", _v)
    b1 = sc.dma("gpsimd", lambda e: e.dma_start(out=red_d, in_=pmax1[:, :]))
    sc.wait("gpsimd", "sg", b1)
    b2 = sc.dma("gpsimd", lambda e: e.dma_start(
        out=pm_row[0:1, :], in_=red_d.rearrange("p one -> one p")))
    sc.wait("vector", "sg", b2)
    sc.slow("vector", lambda e: e.reduce_max(
        gm1[0:1, 0:1], pm_row[0:1, :], axis=AX.X))
    _v = sc.now("v")
    sc.wait("gpsimd", "v", _v)
    b3 = sc.dma("gpsimd", lambda e: e.dma_start(out=cc1i.ap(), in_=gm1[0:1, 0:1]))
    sc.wait("gpsimd", "sg", b3)
    sc.emit("gpsimd", lambda e: e.collective_compute(
        "AllReduce", ALU.max, replica_groups=[list(range(NCORES))],
        ins=[cc1i.ap().opt()], outs=[cc1o.ap().opt()]), sem="cc")
    sc.wait("gpsimd", "cc", sc.now("cc"))
    b4 = sc.dma("gpsimd", lambda e: e.dma_start(
        out=gmx_col[:, :], in_=cc1o.ap().broadcast_to([128, 1])))

    sc.wait("scalar", "si", sc.now("si"))
    sc.emit("scalar", lambda e: e.activation(
        sg_col[:, :], clipx_col[:, :], AF.Sigmoid), sem="s")
    sc.wait("vector", "s", sc.now("s"))
    sc.wait("vector", "sg", b4)
    sc.emit("vector", lambda e: e.scalar_tensor_tensor(
        sx_col[:, :], gmx_col[:, :], 1.0 / QMAX, sg_col[:, :],
        ALU.mult, ALU.mult), sem="v")
    sc.slow("vector", lambda e: e.reciprocal(
        inv_sx_col[:, :], sx_col[:, :]))

    sc.emit("vector", lambda e: e.tensor_scalar(
        qcol[:, :], stats_rinv[:, :], inv_sx_col[:, :], None, ALU.mult), sem="v")
    sc.wait_now("vector", "v")
    xq_tp_arr = [0] * NT
    for j in range(NT):
        sc.emit("vector", lambda e, j=j: e.tensor_scalar(
            q0t[:, :], xc_full[:, j * D:(j + 1) * D], qcol[:, j:j + 1],
            MAGIC, ALU.mult, ALU.add))
        sc.emit("vector", lambda e: e.tensor_scalar(
            q0t[:, :], q0t[:, :], -MAGIC, -QMAX - 1.0, ALU.add, ALU.max))
        if j >= 2:
            sc.wait("vector", "si", xq_tp_arr[j - 2])
        sc.emit("vector", lambda e, j=j: e.tensor_scalar(
            qbt[j % 2][:, :], q0t[:, :], QMAX, None, ALU.min), sem="v")
        _v = sc.now("v")
        sc.wait("sync", "v", _v)
        xq_tp_arr[j] = sc.dma("sync", lambda e, j=j: e.dma_start_transpose(
            xq_fm[:, :].rearrange("p (c t) -> p c t", c=KD)[:, :, j * 128:(j + 1) * 128],
            qbt[j % 2][:, :].rearrange("t (c d) -> t c d", c=KD)))

    sc.wait("gpsimd", "si", xq_tp_arr[NT - 1])
    sc.dma("gpsimd", lambda e: e.dma_start(out=xq_dram, in_=xq_fm[:, :]))

    sc.barrier()
    p0.free()

    pooly = Pool(nc, side="right")
    y_hi = pooly.sb("y_hi", [128, KD * T], BF16)
    y_lo = pooly.sb("y_lo", [128, KD * T], BF16)
    p1 = Pool(nc, side="right")
    wblk = [p1.sb(f"wblk{i}", [128, KD * 128]) for i in range(2)]
    wh = [[p1.sb(f"wh{g}{i}", [128, KD * 128], BF16) for i in range(2)] for g in range(3)]
    wl = [[p1.sb(f"wl{g}{i}", [128, KD * 128], BF16) for i in range(2)] for g in range(3)]
    a_c = p1.sb("a_c", [128, T])
    u_c = p1.sb("u_c", [128, T])
    y_c = p1.sb("y_c", [128, T])
    s1_t = [p1.sb("s1_t0", [128, 512])] * 2
    r_t = [p1.sb("r_t0", [128, 512])] * 2
    v_t = [p1.sb("v_t0", [128, 512])] * 2
    t1_t = [p1.sb("t1_t0", [128, 512])] * 2
    pg = [[p1.ps(f"pg{g}{i}", [128, 512]) for i in range(2)] for g in range(3)]

    win_r = win_d

    wsplit_done = {}
    wsplit_lin = []
    evac_done = {}
    scan_done = [0] * KD
    p1_last_grp = {}

    def p1_split(c):
        for g in range(3):
            eb = g * KD + c
            b = c * 3 + g
            wb = b % 2
            if b >= 2:
                sc.wait("sync", "v", wsplit_lin[b - 2])
            ld = sc.dma("sync", lambda e, wb=wb, eb=eb: e.dma_start(
                out=wblk[wb][:, :].rearrange("p (k e) -> p k e", k=KD),
                in_=win_r[eb]))
            sc.wait("gpsimd", "si", ld)
            if c >= 2:
                sc.wait("gpsimd", "t", p1_last_grp[c - 2])
            cast = sc.dma("gpsimd", lambda e, g=g, c=c, wb=wb: e.dma_start(
                out=wh[g][c % 2][:, :], in_=wblk[wb][:, :]))
            sc.wait("vector", "sg", cast)
            if c >= 2:
                sc.wait("vector", "t", p1_last_grp[c - 2])
            sc.emit("vector", lambda e, g=g, c=c, wb=wb: e.tensor_tensor(
                wl[g][c % 2][:, :], wblk[wb][:, :], wh[g][c % 2][:, :],
                ALU.subtract), sem="v")
            wsplit_done[(c, g)] = sc.now("v")
            wsplit_lin.append(sc.now("v"))

    p1_split(0)
    for c in range(KD):
        for t in range(T4):
            for g in range(3):
                sc.wait("tensor", "v", wsplit_done[(c, g)])
                if c == 0 and t == 0 and g == 0:
                    sc.wait("tensor", "si", xr_tp_done)
                key = (c, t - 2) if t >= 2 else (c - 1, t + 2)
                if key in evac_done:
                    sc.wait("tensor", "s", evac_done[key][0])
                    sc.wait("tensor", "v", evac_done[key][1])
                for pi, (lhs, rhs) in enumerate(
                        [(wh, xrh_fm), (wh, xrl_fm), (wl, xrh_fm)]):
                    for k in range(KD):
                        sc.emit("tensor", lambda e, g=g, c=c, t=t, k=k, pi=pi,
                                lhs=lhs, rhs=rhs: e.matmul(
                            pg[g][t % 2][:, :],
                            lhs[g][c % 2][:, k * 128:(k + 1) * 128],
                            rhs[:, k * T + t * 512:k * T + (t + 1) * 512],
                            start=(pi == 0 and k == 0),
                            stop=(pi == 2 and k == KD - 1)),
                            sem=("t" if (pi == 2 and k == KD - 1) else None))
            grp = sc.now("t")
            sc.wait("scalar", "t", grp)
            if t >= 1:
                sc.wait("scalar", "v", evac_done[(c, t - 1)][1])
            if t == 0 and c >= 1:
                sc.wait("scalar", "v", scan_done[c - 1])
            sl = (t * 512, (t + 1) * 512)
            sc.emit("scalar", lambda e, c=c, t=t, sl=sl: e.activation(
                a_c[:, sl[0]:sl[1]], pg[0][t % 2][:, :], AF.Sigmoid))
            sc.emit("scalar", lambda e, c=c, t=t: e.activation(
                s1_t[t % 2][:, :], pg[0][t % 2][:, :], AF.Sigmoid, scale=-1.0))
            sc.emit("scalar", lambda e, c=c, t=t: e.activation(
                r_t[t % 2][:, :], pg[1][t % 2][:, :], AF.Sigmoid))
            sc.emit("scalar", lambda e, c=c, t=t: e.activation(
                v_t[t % 2][:, :], pg[2][t % 2][:, :], AF.Gelu), sem="s")
            s_ev = sc.now("s")
            sc.wait("vector", "s", s_ev)
            sc.emit("vector", lambda e, t=t: e.tensor_tensor(
                t1_t[t % 2][:, :], r_t[t % 2][:, :], v_t[t % 2][:, :], ALU.mult))
            sc.emit("vector", lambda e, t=t, sl=sl: e.tensor_tensor(
                u_c[:, sl[0]:sl[1]], t1_t[t % 2][:, :], s1_t[t % 2][:, :],
                ALU.mult), sem="v")
            evac_done[(c, t)] = (s_ev, sc.now("v"))
            if t == 0 and c + 1 < KD:
                p1_split(c + 1)
        p1_last_grp[c] = sc.now("t")
        sc.slow("vector", lambda e, c=c: e.tensor_tensor_scan(
            y_c[:, :], a_c[:, :], u_c[:, :], 0.0, ALU.mult, ALU.add))
        sc.emit("vector", lambda e, c=c: e.tensor_copy(
            y_hi[:, c * T:(c + 1) * T], y_c[:, :]))
        sc.emit("vector", lambda e, c=c: e.tensor_tensor(
            y_lo[:, c * T:(c + 1) * T], y_c[:, :], y_hi[:, c * T:(c + 1) * T],
            ALU.subtract), sem="v")
        scan_done[c] = sc.now("v")

    sc.barrier()
    p1.free()
    poolA.free()

    poolg = Pool(nc)
    qwgT = poolg.sb("qwgT", [128, KD * I_], BF16)
    p2a = Pool(nc, side="right")
    woh = p2a.sb("woh", [128, KD * D], BF16)
    wol = p2a.sb("wol", [128, KD * D], BF16)
    attn_sc = [p2a.sb("attn_sc0", [128, D])] * 2
    pw = [p2a.ps(f"pw{i}", [128, D]) for i in range(2)]
    wg_st = [p2a.sb(f"wg_st{i}", [128, D]) for i in range(2)]
    wq0 = p2a.sb("wq0", [128, D])
    wqb = [p2a.sb(f"wqb{i}", [128, D], BF16) for i in range(2)]
    rmaxc = p2a.sb("rmaxc", [128, 2])
    p2w = Pool(nc, side="right")
    wo_st = p2w.sb("wo_st", [128, KD * D])

    sc.wait("scalar", "si", sc.now("si"))
    sc.emit("scalar", lambda e: e.activation(
        lam_col[:, :], lam_col[:, :], AF.Sigmoid), sem="s")
    wor = wout_d.rearrange("(k p) d -> p k d", p=128)
    wo_ld = sc.dma("sync", lambda e: e.dma_start(
        out=wo_st[:, :4 * D].rearrange("p (k d) -> p k d", k=4),
        in_=wor[:, 0:4]))
    wo_ld_b = sc.dma("scalar", lambda e: e.dma_start(
        out=wo_st[:, 4 * D:].rearrange("p (k d) -> p k d", k=4),
        in_=wor[:, 4:8]))
    sc.wait("vector", "si", wo_ld)
    sc.wait("vector", "ss", wo_ld_b)
    sc.wait("vector", "s", sc.now("s"))
    for k in range(KD):
        sc.emit("vector", lambda e, k=k: e.tensor_scalar(
            wo_st[:, k * D:(k + 1) * D], wo_st[:, k * D:(k + 1) * D],
            lam_col[:, k:k + 1], None, ALU.mult),
            sem=("v" if k == KD - 1 else None))
    _v = sc.now("v")
    sc.wait("gpsimd", "v", _v)
    wo_cast = sc.dma("gpsimd", lambda e: e.dma_start(out=woh[:, :], in_=wo_st[:, :]))
    sc.wait("vector", "sg", wo_cast)
    sc.emit("vector", lambda e: e.tensor_tensor(
        wol[:, :], wo_st[:, :], woh[:, :], ALU.subtract), sem="v")
    wo_split = sc.now("v")
    p2w.free()

    poolwd = Pool(nc)
    wd_st2 = poolwd.sb("wd_st2", [128, I_])
    dqb2 = poolwd.sb("dqb2", [128, I_], BF16)
    wd_stage = poolwd.sb("wd_stage", [128, KI * 128], BF16)

    wout_evac = {}
    qwg_tp = [0] * KI

    def p2_wout_step(tc):
        sc.wait("tensor", "v", wo_split)
        if tc >= 2:
            sc.wait("tensor", "v", wout_evac[tc - 2])
        n = 0
        for nh in range(2):
            for pi, (lhs, rhs) in enumerate([(y_hi, woh), (y_hi, wol), (y_lo, woh)]):
                for k in range(KD):
                    n += 1
                    sc.emit("tensor", lambda e, tc=tc, nh=nh, k=k, pi=pi,
                            lhs=lhs, rhs=rhs: e.matmul(
                        pw[tc % 2][:, nh * 512:(nh + 1) * 512],
                        lhs[:, k * T + tc * 128:k * T + (tc + 1) * 128],
                        rhs[:, k * D + nh * 512:k * D + (nh + 1) * 512],
                        start=(pi == 0 and k == 0),
                        stop=(pi == 2 and k == KD - 1)),
                        sem=("t" if (pi == 2 and k == KD - 1) else None))
        grp = sc.now("t")
        sc.wait("vector", "t", grp)
        if tc >= 1:
            sc.wait("vector", "sg", asp_done[tc - 1])
        sc.emit("vector", lambda e, tc=tc: e.tensor_copy(
            attn_sc[tc % 2][:, :], pw[tc % 2][:, :]), sem="v")
        wout_evac[tc] = sc.now("v")
        sc.wait("gpsimd", "v", wout_evac[tc])
        asp_done[tc] = sc.dma("gpsimd", lambda e, tc=tc: e.dma_start(
            out=asp_d[tc], in_=attn_sc[tc % 2][:, :]))

    wgq_v = {}

    def p2_wgq_step(ic):
        if ic >= 2:
            sc.wait("sync", "v", wgq_v[ic - 2])
        ld = sc.dma("sync", lambda e, ic=ic: e.dma_start(
            out=wg_st[ic % 2][:, :], in_=wg_d[ic * 128:(ic + 1) * 128, :]))
        sc.wait("vector", "si", ld)
        sc.slow("vector", lambda e, ic=ic: e.reduce_max(
            rmaxc[:, ic % 2:ic % 2 + 1], wg_st[ic % 2][:, :], axis=AX.X,
            apply_absolute_value=True))
        sc.emit("vector", lambda e, ic=ic: e.tensor_scalar(
            comb_g[:, ic:ic + 1], rmaxc[:, ic % 2:ic % 2 + 1],
            sx_col[:, :], 1.0 / QMAX, ALU.mult, ALU.mult))
        sc.slow("vector", lambda e, ic=ic: e.reciprocal(
            rmaxc[:, ic % 2:ic % 2 + 1], rmaxc[:, ic % 2:ic % 2 + 1]))
        sc.emit("vector", lambda e, ic=ic: e.tensor_scalar(
            wq0[:, :], wg_st[ic % 2][:, :], rmaxc[:, ic % 2:ic % 2 + 1],
            QMAX, ALU.mult, ALU.mult))
        if ic >= 2:
            sc.wait("vector", "si", qwg_tp[ic - 2])
        sc.emit("vector", lambda e, ic=ic: e.tensor_scalar(
            wqb[ic % 2][:, :], wq0[:, :], MAGIC, -MAGIC, ALU.add, ALU.add),
            sem="v")
        wgq_v[ic] = sc.now("v")
        sc.wait("sync", "v", wgq_v[ic])
        qwg_tp[ic] = sc.dma("sync", lambda e, ic=ic: e.dma_start_transpose(
            qwgT[:, :].rearrange("p (k i) -> p k i", k=KD)[:, :, ic * 128:(ic + 1) * 128],
            wqb[ic % 2][:, :].rearrange("i (k h) -> i k h", k=KD)))

    wdq_done = [0] * KD
    wdtp_ss = [0] * KD
    wdsp_ss = [0] * KD

    def p2_wdred_step(oc):
        if oc == 0:
            sc.wait("scalar", "v", wo_split)
        if oc >= 1:
            sc.wait("scalar", "v", wdq_done[oc - 1])
        ldw = sc.dma("scalar", lambda e, oc=oc: e.dma_start(
            out=wd_st2[:, :], in_=wd_d[oc * 128:(oc + 1) * 128, :]))
        sc.wait("vector", "ss", ldw)
        sc.slow("vector", lambda e, oc=oc: e.reduce_max(
            drmax8_p[:, oc:oc + 1], wd_st2[:, :], axis=AX.X,
            apply_absolute_value=True))
        sc.emit("vector", lambda e, oc=oc: e.tensor_scalar(
            comb_d[:, oc:oc + 1], drmax8_p[:, oc:oc + 1], 1.0, None, ALU.mult))
        sc.emit("vector", lambda e, oc=oc: e.tensor_scalar(
            rinv127_p[:, oc:oc + 1], drmax8_p[:, oc:oc + 1], 1.0 / QMAX, None,
            ALU.mult))
        sc.slow("vector", lambda e, oc=oc: e.reciprocal(
            rinv127_p[:, oc:oc + 1], rinv127_p[:, oc:oc + 1]))
        sc.emit("vector", lambda e, oc=oc: e.tensor_scalar(
            wd_st2[:, :], wd_st2[:, :], rinv127_p[:, oc:oc + 1], MAGIC,
            ALU.mult, ALU.add))
        if oc >= 1:
            sc.wait("vector", "ss", wdtp_ss[oc - 1])
        sc.emit("vector", lambda e, oc=oc: e.tensor_scalar(
            dqb2[:, :], wd_st2[:, :], -MAGIC, None, ALU.add), sem="v")
        wdq_done[oc] = sc.now("v")
        sc.wait("scalar", "v", wdq_done[oc])
        if oc >= 1:
            sc.wait("scalar", "ss", wdsp_ss[oc - 1])
        wdtp_ss[oc] = sc.dma("scalar", lambda e, oc=oc: e.dma_start_transpose(
            wd_stage[:, :].rearrange("p (k o) -> p k o", k=KI),
            dqb2[:, :].rearrange("o (k h) -> o k h", k=KI)))
        sc.wait("scalar", "ss", wdtp_ss[oc])
        wdsp_ss[oc] = sc.dma("scalar", lambda e, oc=oc: e.dma_start(
            out=qwdT_d.rearrange("p (k o) -> p k o", k=KI)[:, :, oc * 128:(oc + 1) * 128],
            in_=wd_stage[:, :].rearrange("p (k o) -> p k o", k=KI)))

    asp_done = {}
    steps = []
    for i in range(max(NT, KI)):
        if i < NT:
            steps.append(("w", i))
        if i < KI:
            steps.append(("q", i))
        if i % 2 == 1 and i // 2 < KD:
            steps.append(("d", i // 2))
    for kind, i in steps:
        if kind == "w":
            p2_wout_step(i)
        elif kind == "q":
            p2_wgq_step(i)
        else:
            p2_wdred_step(i)

    sc.barrier(sems=("si", "sg", "v", "s", "t"))
    p2a.free()
    pooly.free()

    poolxq2 = Pool(nc)
    xq_fm2 = poolxq2.sb("xq_fm2", [128, KD * T], BF16)
    xq_ld = sc.dma("gpsimd", lambda e: e.dma_start(out=xq_fm2[:, :], in_=xq_dram))
    p3 = Pool(nc, side="right")
    qwuT = p3.sb("qwuT", [128, KD * I_], BF16)
    wu_st = [p3.sb(f"wu_st{i}", [128, D]) for i in range(2)]
    uq0 = p3.sb("uq0", [128, D])
    uqb = [p3.sb(f"uqb{i}", [128, D], BF16) for i in range(2)]
    urmax = p3.sb("urmax", [128, 2])
    sg_t = [p3.sb(f"sg_t{i}", [128, 512]) for i in range(2)]
    uu_t = [p3.sb(f"uu_t{i}", [128, 512]) for i in range(2)]
    h_ch = [p3.sb(f"h_ch{i}", [128, T]) for i in range(2)]
    hrun = p3.sb("hrun", [128, 1])
    habs = p3.sb("habs", [128, 1])
    pgu = [[p3.ps(f"pgu{a}{i}", [128, 512]) for i in range(2)] for a in range(2)]

    sc.emit("vector", lambda e: e.memset(hrun[:, :], 0.0))

    qwu_tp = [0] * KI
    uq_v = {}

    def p3_wuq_step(ic):
        if ic >= 2:
            sc.wait("sync", "v", uq_v[ic - 2])
        ld = sc.dma("sync", lambda e, ic=ic: e.dma_start(
            out=wu_st[ic % 2][:, :], in_=wu_d[ic * 128:(ic + 1) * 128, :]))
        sc.wait("vector", "si", ld)
        sc.slow("vector", lambda e, ic=ic: e.reduce_max(
            urmax[:, ic % 2:ic % 2 + 1], wu_st[ic % 2][:, :], axis=AX.X,
            apply_absolute_value=True))
        sc.emit("vector", lambda e, ic=ic: e.tensor_scalar(
            comb_u[:, ic:ic + 1], urmax[:, ic % 2:ic % 2 + 1],
            sx_col[:, :], 1.0 / QMAX, ALU.mult, ALU.mult))
        sc.slow("vector", lambda e, ic=ic: e.reciprocal(
            urmax[:, ic % 2:ic % 2 + 1], urmax[:, ic % 2:ic % 2 + 1]))
        sc.emit("vector", lambda e, ic=ic: e.tensor_scalar(
            uq0[:, :], wu_st[ic % 2][:, :], urmax[:, ic % 2:ic % 2 + 1],
            QMAX, ALU.mult, ALU.mult))
        if ic >= 2:
            sc.wait("vector", "si", qwu_tp[ic - 2])
        sc.emit("vector", lambda e, ic=ic: e.tensor_scalar(
            uqb[ic % 2][:, :], uq0[:, :], MAGIC, -MAGIC, ALU.add, ALU.add),
            sem="v")
        uq_v[ic] = sc.now("v")
        sc.wait("sync", "v", uq_v[ic])
        qwu_tp[ic] = sc.dma("sync", lambda e, ic=ic: e.dma_start_transpose(
            qwuT[:, :].rearrange("p (k i) -> p k i", k=KD)[:, :, ic * 128:(ic + 1) * 128],
            uqb[ic % 2][:, :].rearrange("i (k h) -> i k h", k=KD)))

    p3_ev = {}
    p3_tt = {}
    hsp_done = {}
    for ic in range(KI):
        if ic == 0:
            p3_wuq_step(0)
        for t in range(T4):
            sc.wait("tensor", "si", qwu_tp[ic])
            if ic == 0 and t == 0:
                sc.wait("tensor", "sg", xq_ld)
            key = (ic, t - 2) if t >= 2 else (ic - 1, t + 2)
            if key in p3_ev:
                sc.wait("tensor", "s", p3_ev[key])
            for ai, wT in enumerate((qwgT, qwuT)):
                for k in range(KD):
                    sc.emit("tensor", lambda e, ic=ic, t=t, k=k, ai=ai, wT=wT:
                            e.matmul(
                        pgu[ai][t % 2][:, :],
                        wT[:, k * I_ + ic * 128:k * I_ + (ic + 1) * 128],
                        xq_fm2[:, k * T + t * 512:k * T + (t + 1) * 512],
                        start=(k == 0), stop=(k == KD - 1)),
                        sem=("t" if (ai == 1 and k == KD - 1) else None))
            grp = sc.now("t")
            sc.wait("scalar", "t", grp)
            if key in p3_tt:
                sc.wait("scalar", "v", p3_tt[key])
            sc.emit("scalar", lambda e, ic=ic, t=t: e.activation(
                sg_t[t % 2][:, :], pgu[0][t % 2][:, :], AF.Silu,
                scale=comb_g[:, ic:ic + 1]))
            sc.emit("scalar", lambda e, ic=ic, t=t: e.activation(
                uu_t[t % 2][:, :], pgu[1][t % 2][:, :], AF.Identity,
                scale=comb_u[:, ic:ic + 1]), sem="s")
            p3_ev[(ic, t)] = sc.now("s")
            sc.wait("vector", "s", p3_ev[(ic, t)])
            if t == 0 and ic >= 2:
                sc.wait("vector", "sg", hsp_done[ic - 2])
            sc.emit("vector", lambda e, ic=ic, t=t: e.tensor_tensor(
                h_ch[ic % 2][:, t * 512:(t + 1) * 512], sg_t[t % 2][:, :],
                uu_t[t % 2][:, :], ALU.mult), sem="v")
            p3_tt[(ic, t)] = sc.now("v")
            if t == 0 and ic + 1 < KI:
                p3_wuq_step(ic + 1)
        sc.slow("vector", lambda e, ic=ic: e.reduce_max(
            habs[:, :], h_ch[ic % 2][:, :], axis=AX.X,
            apply_absolute_value=True))
        sc.emit("vector", lambda e: e.scalar_tensor_tensor(
            hrun[:, :], habs[:, :], 1.0, hrun[:, :], ALU.mult, ALU.max),
            sem="v")
        _v = sc.now("v")
        sc.wait("gpsimd", "v", _v)
        hsp_done[ic] = sc.dma("gpsimd", lambda e, ic=ic: e.dma_start(
            out=hsp_d.rearrange("g p x -> p g x")[:, :, ic * 256:(ic + 1) * 256],
            in_=h_ch[ic % 2][:, :].rearrange("p (g t) -> p g t", g=T // 256)))

    _v = sc.now("v")
    sc.wait("gpsimd", "v", _v)
    c1 = sc.dma("gpsimd", lambda e: e.dma_start(out=red_d, in_=hrun[:, :]))
    sc.wait("gpsimd", "sg", c1)
    c2 = sc.dma("gpsimd", lambda e: e.dma_start(
        out=pm_row[0:1, :], in_=red_d.rearrange("p one -> one p")))
    sc.wait("vector", "sg", c2)
    sc.slow("vector", lambda e: e.reduce_max(
        gm1[0:1, 0:1], pm_row[0:1, :], axis=AX.X))
    _v = sc.now("v")
    sc.wait("gpsimd", "v", _v)
    c3 = sc.dma("gpsimd", lambda e: e.dma_start(out=cc2i.ap(), in_=gm1[0:1, 0:1]))
    sc.wait("gpsimd", "sg", c3)
    sc.emit("gpsimd", lambda e: e.collective_compute(
        "AllReduce", ALU.max, replica_groups=[list(range(NCORES))],
        ins=[cc2i.ap().opt()], outs=[cc2o.ap().opt()]), sem="cc")
    cc2_done = sc.now("cc")

    sc.barrier()
    p3.free()
    poolxq2.free()
    poolwd.free()
    poolg.free()

    p4 = Pool(nc, side="right")
    qwdT = p4.sb("qwdT", [128, KI * D], BF16)
    h_g = [p4.sb(f"h_g{i}", [128, KI * 256]) for i in range(2)]
    qh_g = [p4.sb(f"qh_g{i}", [128, KI * 256], BF16) for i in range(2)]
    x_t4 = [p4.sb(f"x_t4{i}", [128, D]) for i in range(2)]
    xc_t4 = p4.sb("xc_t4", [128, D])
    xn_t4 = [p4.sb(f"xn_t4{i}", [128, D]) for i in range(2)]
    at_t4 = [p4.sb(f"at_t4{i}", [128, D]) for i in range(2)]
    o_t4 = [p4.sb(f"o_t4{i}", [128, D]) for i in range(2)]
    pd = [p4.ps(f"pd{i}", [128, D]) for i in range(2)]

    NG = T // 256

    qwd_a = sc.dma("sync", lambda e: e.dma_start(
        out=qwdT[:, 0:11 * D], in_=qwdT_d[:, 0:11 * D]))
    qwd_b = sc.dma("gpsimd", lambda e: e.dma_start(
        out=qwdT[:, 11 * D:], in_=qwdT_d[:, 11 * D:]))

    sc.wait("gpsimd", "cc", cc2_done)
    c4 = sc.dma("gpsimd", lambda e: e.dma_start(
        out=gmx_col[:, :], in_=cc2o.ap().broadcast_to([128, 1])))
    sc.emit("scalar", lambda e: e.activation(
        sg_col[:, :], clipd_col[:, :], AF.Sigmoid), sem="s")
    sc.wait("vector", "s", sc.now("s"))
    sc.wait("vector", "sg", c4)
    sc.emit("vector", lambda e: e.scalar_tensor_tensor(
        sh_col[:, :], gmx_col[:, :], 1.0 / QMAX, sg_col[:, :],
        ALU.mult, ALU.mult), sem="v")
    sc.slow("vector", lambda e: e.reciprocal(
        inv_sh_col[:, :], sh_col[:, :]))

    hq_g = [0] * NG
    grp_g = [0] * NG
    hgl_arr = [0] * NG

    def h_load(g):
        if g >= 2:
            sc.wait("scalar", "v", hq_g[g - 2])
        hgl_arr[g] = sc.dma("scalar", lambda e, g=g: e.dma_start(
            out=h_g[g % 2][:, :], in_=hsp_d[g]))

    def h_quant(g):
        sc.wait("vector", "ss", hgl_arr[g])
        if g >= 2:
            sc.wait("vector", "t", grp_g[g - 2])
        sc.emit("vector", lambda e, g=g: e.tensor_scalar(
            h_g[g % 2][:, :], h_g[g % 2][:, :], inv_sh_col[:, :], MAGIC,
            ALU.mult, ALU.add))
        sc.emit("vector", lambda e, g=g: e.tensor_scalar(
            h_g[g % 2][:, :], h_g[g % 2][:, :], -MAGIC, -QMAX - 1.0,
            ALU.add, ALU.max))
        sc.emit("vector", lambda e, g=g: e.tensor_scalar(
            qh_g[g % 2][:, :], h_g[g % 2][:, :], QMAX, None, ALU.min), sem="v")
        hq_g[g] = sc.now("v")

    h_load(0)
    h_load(1)

    sc.emit("vector", lambda e: e.tensor_scalar(
        comb_d[:, :], comb_d[:, :], sh_col[:, :], 1.0 / QMAX,
        ALU.mult, ALU.mult), sem="v")


    _v = sc.now("v")
    sc.wait("gpsimd", "v", _v)
    d1b = sc.dma("gpsimd", lambda e: e.dma_start(
        out=os_d.rearrange("c p -> p c"), in_=comb_d[:, :],
        allow_slow_non_contiguous=True))
    sc.wait("gpsimd", "sg", d1b)
    d2b = sc.dma("gpsimd", lambda e: e.dma_start(
        out=oscale_bc[:, :],
        in_=os_d.rearrange("c p -> (c p)").rearrange("(a d) -> a d", a=1)
        .broadcast_to([128, D])))
    sc.wait("vector", "sg", d2b)

    h_quant(0)
    p4_ev = {}
    out_done = {}
    for g in range(NG):
        if g + 1 < NG:
            h_load(g + 1)
        for s in range(2):
            tc = g * 2 + s
            if tc >= 2:
                sc.wait("sync", "v", p4_ev[tc - 2][1])
            xld = sc.dma("sync", lambda e, tc=tc: e.dma_start(
                out=x_t4[tc % 2][:, :], in_=x_d[tc * 128:(tc + 1) * 128, :]))
            if tc >= 2:
                sc.wait("gpsimd", "v", p4_ev[tc - 2][1])
            ald = sc.dma("gpsimd", lambda e, tc=tc: e.dma_start(
                out=at_t4[tc % 2][:, :], in_=asp_d[tc]))
            sc.wait("tensor", "v", hq_g[g])
            if tc == 0:
                sc.wait("tensor", "si", qwd_a)
                sc.wait("tensor", "sg", qwd_b)
            if tc >= 2:
                sc.wait("tensor", "v", p4_ev[tc - 2][1])
            for nh in range(2):
                for k in range(KI):
                    sc.emit("tensor", lambda e, s=s, g=g, tc=tc, nh=nh, k=k: e.matmul(
                        pd[tc % 2][:, nh * 512:(nh + 1) * 512],
                        qh_g[g % 2][:, k * 256 + s * 128:k * 256 + (s + 1) * 128],
                        qwdT[:, k * D + nh * 512:k * D + (nh + 1) * 512],
                        start=(k == 0), stop=(k == KI - 1)),
                        sem=("t" if (nh == 1 and k == KI - 1) else None))
            grp = sc.now("t")
            grp_g[g] = grp
            if s == 0 and g + 1 < NG:
                h_quant(g + 1)
            sc.wait("scalar", "si", xld)
            sc.emit("scalar", lambda e, tc=tc: e.activation(
                xc_t4[:, :], x_t4[tc % 2][:, :], AF.Identity,
                bias=stats_negm[:, tc:tc + 1]))
            sc.emit("scalar", lambda e, tc=tc: e.activation(
                xn_t4[tc % 2][:, :], xc_t4[:, :], AF.Identity,
                scale=stats_rinv[:, tc:tc + 1]), sem="s")
            sres = sc.now("s")
            sc.wait("vector", "t", grp)
            sc.wait("vector", "s", sres)
            sc.wait("vector", "sg", ald)
            if tc >= 2:
                sc.wait("vector", "si", out_done[tc - 2])
            sc.emit("vector", lambda e, tc=tc: e.tensor_tensor(
                xn_t4[tc % 2][:, :], xn_t4[tc % 2][:, :], x_t4[tc % 2][:, :], ALU.add))
            sc.emit("vector", lambda e, tc=tc: e.tensor_tensor(
                o_t4[tc % 2][:, :], pd[tc % 2][:, :], oscale_bc[:, :], ALU.mult))
            sc.emit("vector", lambda e, tc=tc: e.tensor_tensor(
                o_t4[tc % 2][:, :], o_t4[tc % 2][:, :], at_t4[tc % 2][:, :], ALU.add))
            sc.emit("vector", lambda e, tc=tc: e.tensor_tensor(
                o_t4[tc % 2][:, :], o_t4[tc % 2][:, :], xn_t4[tc % 2][:, :], ALU.add),
                sem="v")
            p4_ev[tc] = (grp, sc.now("v"))
            sc.wait("sync", "v", p4_ev[tc][1])
            out_done[tc] = sc.dma("sync", lambda e, tc=tc: e.dma_start(
                out=out_d[tc * 128:(tc + 1) * 128, :], in_=o_t4[tc % 2][:, :]))

    sc.wait_now("sync", "si")
    sc.wait_now("gpsimd", "sg")

    semnames = ["si", "sg", "ss", "v", "s", "t", "cc"]
    for n in semnames:
        g = nc.semaphore(n)
        sc.semh[n] = g.__enter__()
    with nc.Block() as block:
        @block.sync
        def _(eng):
            for f in sc.ops["sync"]:
                f(eng)

        @block.vector
        def _(eng):
            for f in sc.ops["vector"]:
                f(eng)

        @block.scalar
        def _(eng):
            for f in sc.ops["scalar"]:
                f(eng)

        @block.tensor
        def _(eng):
            for f in sc.ops["tensor"]:
                f(eng)

        @block.gpsimd
        def _(eng):
            for f in sc.ops["gpsimd"]:
                f(eng)
    perm.free()
    p4.free()
    return nc


def kernel(**inputs):
    f32 = lambda k: np.ascontiguousarray(np.asarray(inputs[k], dtype=np.float32))
    x = f32("L0_attn_x")
    shared = {
        "L0_attn_w_norm": f32("L0_attn_w_norm"),
        "L0_attn_W_in": np.ascontiguousarray(
            f32("L0_attn_W_in").reshape(KD, 128, KE, 128).transpose(2, 1, 0, 3)),
        "L0_attn_lambda_log": f32("L0_attn_lambda_log"),
        "L0_attn_W_out": f32("L0_attn_W_out"),
        "L0_ffn_clip_x": f32("L0_ffn_clip_x"),
        "L0_ffn_clip_down": f32("L0_ffn_clip_down"),
        "L0_ffn_w_gate": f32("L0_ffn_w_gate"),
        "L0_ffn_w_up": f32("L0_ffn_w_up"),
        "L0_ffn_w_down": f32("L0_ffn_w_down"),
    }
    nc = build()
    in_maps = [dict(shared, L0_attn_x=np.ascontiguousarray(x[c]))
               for c in range(NCORES)]
    res = run_bass_kernel_spmd(nc, in_maps, core_ids=list(range(NCORES)))
    return np.stack(
        [np.asarray(r["out"], dtype=np.float32) for r in res.results], axis=0)
